# version 04
# baseline (speedup 1.0000x reference)
import numpy as np
import ml_dtypes
import concourse.bass as bass
import concourse.mybir as mybir
from concourse.bass_utils import run_bass_kernel_spmd
from concourse.bass_core_rust import IndirectOffsetOnAxis

F32 = mybir.dt.float32
BF16 = mybir.dt.bfloat16
U32 = mybir.dt.uint32
AF = mybir.ActivationFunctionType
ALU = mybir.AluOpType
AX = mybir.AxisListType

T, D, FF, E = 2048, 1024, 4096, 8
NT = T // 128
ND = D // 128
NTC = 4
NSLOT = 40
SBUF_CAP = 207000

ENGINES = ("sync", "gpsimd", "tensor", "vector", "scalar")


def slot(tc, sc):
    return 2 * tc * tc + 2 * tc + sc


class Prog:
    def __init__(self, nc):
        self.nc = nc
        self.ops = {e: [] for e in ENGINES}
        self.sem = {}
        self.cnt = {}
        for name in ("q", "g", "t", "v", "s"):
            self.sem[name] = nc.alloc_semaphore("sem_" + name)
            self.cnt[name] = 0

    def emit(self, eng, fn=None, inc=None, wait=(), dma=False):
        w = []
        for ev in wait:
            if ev is None:
                continue
            sname, val = ev
            assert val <= self.cnt[sname], (sname, val, self.cnt[sname])
            if val > 0:
                w.append((self.sem[sname], val))
        ev_out = None
        n = 16 if dma else 1
        if inc is not None:
            self.cnt[inc] += n
            ev_out = (inc, self.cnt[inc])
        self.ops[eng].append((w, fn, self.sem[inc] if inc else None, n))
        return ev_out

    def ev(self, sname):
        return (sname, self.cnt[sname])

    def into_block(self, block):
        def mk(eng_name):
            def body(eng):
                for (waits, fn, sem, n) in self.ops[eng_name]:
                    for (s, v) in waits:
                        eng.wait_ge(s, v)
                    inst = fn(eng) if fn is not None else None
                    if sem is not None:
                        assert inst is not None, eng_name
                        inst.then_inc(sem, n)
            return body
        for e in ENGINES:
            getattr(block, e)(mk(e))


class Alloc:
    def __init__(self, nc, cap=None):
        self.nc = nc
        self.base = (nc.sbuf_base + 63) // 64 * 64
        self.cap = cap if cap is not None else nc.sbuf_top
        self.items = []

    def at(self, name, shape, dtype, ph0, ph1, align=64):
        nb = int(np.prod(shape[1:])) * mybir.dt.size(dtype)
        occ = sorted((s, e) for (s, e, p0, p1, _) in self.items
                     if not (p1 < ph0 or p0 > ph1))
        pos = self.base
        for (s, e) in occ:
            if pos + nb <= s:
                break
            pos = max(pos, e)
            pos = (pos + align - 1) // align * align
        assert pos + nb <= self.cap, (name, pos, nb)
        self.items.append((pos, pos + nb, ph0, ph1, name))
        return self.nc.alloc_sbuf_tensor_at(name, list(shape), dtype, offset=pos).ap()


def build_nc(debug_outputs=(), stop_after=None):
    nc = bass.Bass("TRN2", target_bir_lowering=False)

    xT_h = nc.dram_tensor("xT", [D, T], F32, kind="ExternalInput").ap()
    x_h = nc.dram_tensor("x", [T, D], F32, kind="ExternalInput").ap()
    WT_h = {p: nc.dram_tensor(f"W{p}T", [D, D], F32, kind="ExternalInput").ap()
            for p in ("q", "k", "v", "a")}
    b128_h = {p: nc.dram_tensor(f"b{p}128", [128, 8], F32, kind="ExternalInput").ap()
              for p in ("q", "k", "a")}
    bv_h = nc.dram_tensor("bvrow", [1, D], F32, kind="ExternalInput").ap()
    Wr_h = nc.dram_tensor("Wr128", [128, 64], F32, kind="ExternalInput").ap()
    b1_h = nc.dram_tensor("b1", [E, FF], F32, kind="ExternalInput").ap()
    b2_h = nc.dram_tensor("b2", [E, D], F32, kind="ExternalInput").ap()
    W1_h = nc.dram_tensor("W1", [E, D, FF], F32, kind="ExternalInput").ap()
    W2_h = nc.dram_tensor("W2", [E, FF, D], F32, kind="ExternalInput").ap()
    W1flat = W1_h.rearrange("e d f -> (e d) f")
    W2flat = W2_h.rearrange("e f d -> (e f) d")
    ident_h = nc.dram_tensor("ident", [128, 128], BF16, kind="ExternalInput").ap()
    masks_h = nc.dram_tensor("masks", [128, 2048], BF16, kind="ExternalInput").ap()
    cst_h = nc.dram_tensor("cst", [128, 8], F32, kind="ExternalInput").ap()
    iw1_h = nc.dram_tensor("iotaW1", [128, 8], U32, kind="ExternalInput").ap()
    iw2_h = nc.dram_tensor("iotaW2", [128, 32], U32, kind="ExternalInput").ap()
    i8r_h = nc.dram_tensor("iota8row", [1, 8], F32, kind="ExternalInput").ap()
    icol_h = nc.dram_tensor("iotacol", [128, 1], F32, kind="ExternalInput").ap()
    big_h = nc.dram_tensor("bigrow", [1, 8], F32, kind="ExternalInput").ap()
    ones_h = nc.dram_tensor("onespack", [128, 4], F32, kind="ExternalInput").ap()
    onesrow_h = nc.dram_tensor("onesrow", [1, 512], F32, kind="ExternalInput").ap()

    out_h = nc.dram_tensor("out", [T, D], F32, kind="ExternalOutput").ap()

    al = Alloc(nc)
    C = lambda name, shape, dt: al.at(name, shape, dt, 0, 99)
    masks = C("c_masks", [128, 2048], BF16)
    cst = C("c_cst", [128, 8], F32)
    iw1 = C("c_iw1", [128, 8], U32)
    iw2 = C("c_iw2", [128, 32], U32)
    i8r = C("c_i8r", [1, 8], F32)
    icol = C("c_icol", [128, 1], F32)
    bigr = C("c_big", [1, 8], F32)
    onescol = C("c_onescol", [128, 4], F32)
    onescol_bf = C("c_onescol_bf", [128, 4], BF16)
    onesrow = C("c_onesrow", [1, 512], F32)
    onesrow_bf = C("c_onesrow_bf", [1, 512], BF16)
    bq = C("c_bq", [128, 8], F32)
    bk = C("c_bk", [128, 8], F32)
    ba = C("c_ba", [128, 8], F32)
    Wr = C("c_wr", [128, 64], F32)
    rsc = C("r_sc", [1, 64], F32)
    rcol = C("r_col", [128, 16], F32)
    ridx1 = C("r_idx1", [128, 16], U32)
    ridx2 = C("r_idx2", [128, 64], U32)
    oh8 = C("r_oh8", [128, 4], BF16)
    lnsc = C("r_lnsc", [128, 8], F32)
    rmsk = C("r_msk", [1, 8], U32)
    lnsc2 = C("r_lnsc2", [128, 8], F32)
    sqd3 = al.at("sqd3", [128, D], F32, 5, 6)
    ident_sb = C("c_ident", [128, 128], BF16)

    xT = al.at("xT", [128, ND * T], BF16, 1, 1)
    qT = al.at("qT", [128, ND * T], BF16, 1, 2)
    y_acc = al.at("y_acc", [128, NT * D], F32, 3, 6)
    WTb = [al.at(f"WTb{i}", [128, ND * D], BF16, 1, 1) for i in range(2)]
    WTq = al.at("WTq", [128, ND * D], BF16, 1, 1)
    scoresT = al.at("scoresT", [128, NSLOT * 512], BF16, 2, 3)
    x1T = al.at("x1T", [128, ND * T], BF16, 4, 6)
    b1b = al.at("b1b", [128, FF], BF16, 3, 5)
    wkT = al.at("wkT", [128, ND * T], BF16, 1, 2)
    W1q = [None, None]
    W2q = [None, None]
    W1q[0] = al.at("W1qA", [128, ND * 1024], BF16, 4, 6)
    W2q[0] = al.at("W2qA", [128, 8 * D], BF16, 4, 6)
    vS = al.at("v", [128, NT * D], BF16, 1, 3)
    W1q[1] = al.at("W1qB", [128, ND * 1024], BF16, 4, 6)
    W2q[1] = al.at("W2qB", [128, 8 * D], BF16, 4, 6)
    Abuf = al.at("Abuf", [128, T], F32, 1, 1)
    Gbuf = al.at("Gbuf", [128, T], BF16, 1, 1)
    bvrow = al.at("bvrow", [1, D], F32, 1, 1)
    hbuf = al.at("hbuf", [128, D], F32, 3, 3)
    xres = al.at("xres", [128, 2 * D], F32, 3, 3)
    sqd = al.at("sqd", [128, D], F32, 3, 3)
    hsb = [al.at("hsbA", [128, 512], BF16, 4, 5),
           al.at("hsbB", [128, 512], BF16, 4, 5)]
    x1btmp = al.at("x1btmp", [128, D], BF16, 4, 4)
    b2b = al.at("b2b", [128, D], BF16, 3, 5)
    b2row = al.at("b2row", [1, D], BF16, 4, 5)
    b1selT = al.at("b1selT", [128, 64], F32, 4, 5)
    outst = al.at("outst", [128, 2 * D], F32, 4, 6)
    sqd2 = al.at("sqd2", [128, D], F32, 5, 6)

    pspr = nc.alloc_psum_tensor("pspr", [128, 2048], F32).ap()
    psnum = nc.alloc_psum_tensor("psnum", [128, 1024], F32).ap()
    psx = nc.alloc_psum_tensor("psx", [128, 1024], F32).ap()

    P = Prog(nc)
    DBG = {}

    def finish():
        for name in debug_outputs:
            ap, shape, dt, waits = DBG[name]
            dh = nc.dram_tensor("dbg_" + name, list(shape), dt, kind="ExternalOutput").ap()
            P.emit("sync", lambda e, o=dh, i=ap: e.dma_start(out=o, in_=i),
                   inc="q", dma=True, wait=waits)
        P.emit("sync", None, wait=[P.ev("q"), P.ev("g")])
        with nc.Block() as block:
            P.into_block(block)
        return nc

    def ld(engq, out_ap, in_ap, semname, wait=()):
        return P.emit(engq, lambda e, o=out_ap, i=in_ap: e.dma_start(out=o, in_=i),
                      inc=semname, dma=True, wait=wait)

    for o, i in ((masks, masks_h), (cst, cst_h), (iw1, iw1_h),
                 (iw2, iw2_h), (i8r, i8r_h), (icol, icol_h), (bigr, big_h),
                 (onescol, ones_h), (onesrow, onesrow_h),
                 (bq, b128_h["q"]), (bk, b128_h["k"]), (ba, b128_h["a"]),
                 (Wr, Wr_h), (bvrow, bv_h)):
        ld("sync", o, i, "q")
    ev_ident = ld("sync", ident_sb, ident_h, "q")
    ld("gpsimd", onescol_bf, ones_h, "g")
    ld("gpsimd", onesrow_bf, onesrow_h, "g")
    ev_consts_q = P.ev("q")
    ev_WT = {}
    ev_WT["a"] = ld("gpsimd", WTb[0].rearrange("p (c i) -> p c i", c=ND),
                    WT_h["a"].rearrange("(c p) i -> p c i", p=128), "g")
    ev_xTs = [None] * NTC
    xT3 = xT.rearrange("p (c t) -> p c t", c=ND)
    xTh3 = xT_h.rearrange("(c p) t -> p c t", p=128)
    for tt in range(NTC):
        ev_xTs[tt] = ld("gpsimd", xT3[:, :, tt * 512:(tt + 1) * 512],
                        xTh3[:, :, tt * 512:(tt + 1) * 512], "g")
        if tt == 1:
            ev_WT["k"] = ld("gpsimd", WTb[1].rearrange("p (c i) -> p c i", c=ND),
                            WT_h["k"].rearrange("(c p) i -> p c i", p=128), "g")
    ev_xT = ev_xTs[NTC - 1]
    ev_WT["qp"] = ld("gpsimd", WTq.rearrange("p (c i) -> p c i", c=ND),
                     WT_h["q"].rearrange("(c p) i -> p c i", p=128), "g")
    ev_consts_g = P.ev("g")

    bank_ev = [None, None, None, None]

    def proj_mm(wt_ap, ci, tt, bank, wait):
        ps = pspr[:, bank * 512:(bank + 1) * 512]

        def fn(e, wt=wt_ap, ci=ci, tt=tt, ps=ps):
            inst = None
            for cd in range(ND):
                inst = e.matmul(ps, wt[:, cd * D + ci * 128: cd * D + ci * 128 + 128],
                                xT[:, cd * T + tt * 512: cd * T + tt * 512 + 512],
                                start=(cd == 0), stop=(cd == ND - 1))
            return inst
        return P.emit("tensor", fn, inc="t", wait=wait), ps

    ev_scan_prev = None
    for ci in range(ND):
        for which in ("a", "k"):
            wt_ap = WTb[1] if which == "k" else WTb[0]
            for tt in range(NTC):
                bank = (ci * 8 + (0 if which == "k" else 4) + tt) % 4
                ev_mm, ps = proj_mm(wt_ap, ci, tt, bank,
                                    [ev_WT[which], ev_xTs[tt], ev_consts_q, bank_ev[bank]])
                if which == "k":
                    dst = wkT[:, ci * T + tt * 512: ci * T + tt * 512 + 512]
                    bank_ev[bank] = P.emit(
                        "vector",
                        lambda e, d=dst, p=ps, c=ci: e.tensor_scalar(
                            d, p, bk[:, c:c + 1], 0.0, op0=ALU.add, op1=ALU.max),
                        inc="v", wait=[ev_mm])
                else:
                    dst = Abuf[:, tt * 512:(tt + 1) * 512]
                    bank_ev[bank] = P.emit(
                        "scalar",
                        lambda e, d=dst, p=ps, c=ci: e.activation(
                            out=d, in_=p, func=AF.Sigmoid, bias=ba[:, c:c + 1]),
                        inc="s", wait=[ev_mm, ev_scan_prev if tt == 0 else None])
        ev_sig = P.ev("s")
        ev_eps_prev = P.emit("vector",
                             lambda e: e.tensor_scalar_add(Abuf[:], Abuf[:], 1e-6),
                             inc="v", wait=[ev_sig])
        ev_scan = P.emit("vector",
                         lambda e: e.tensor_tensor_scan(
                             Gbuf[:, ::-1], Abuf[:, ::-1], Abuf[:, ::-1], 1.0,
                             ALU.mult, ALU.bypass),
                         inc="v", wait=[ev_eps_prev])
        ev_scan_prev = ev_scan
        P.emit("vector",
               lambda e, c=ci: e.tensor_mul(wkT[:, c * T:(c + 1) * T],
                                            wkT[:, c * T:(c + 1) * T], Gbuf[:]),
               inc="v", wait=[ev_scan])
    ev_wk_done = P.ev("v")

    ev_ak_mm = P.ev("t")
    ev_WT["vp"] = ld("gpsimd", WTb[1].rearrange("p (c i) -> p c i", c=ND),
                     WT_h["v"].rearrange("(c p) i -> p c i", p=128), "g",
                     wait=[ev_ak_mm])

    for ci in range(ND):
        for tt in range(NTC):
            bank = (ci * 4 + tt) % 4
            ev_mm, ps = proj_mm(WTq, ci, tt, bank, [ev_WT["qp"], bank_ev[bank]])
            dst = qT[:, ci * T + tt * 512: ci * T + tt * 512 + 512]
            bank_ev[bank] = P.emit(
                "vector",
                lambda e, d=dst, p=ps, c=ci: e.tensor_scalar(
                    d, p, bq[:, c:c + 1], 0.0, op0=ALU.add, op1=ALU.max),
                inc="v", wait=[ev_mm])
    for ti in range(NT):
        for jc in range(2):
            bank = (ti * 2 + jc) % 4
            ps = pspr[:, bank * 512:(bank + 1) * 512]

            def fnv(e, ti=ti, jc=jc, ps=ps):
                for cd in range(ND):
                    e.matmul(ps, xT[:, cd * T + ti * 128: cd * T + ti * 128 + 128],
                             WTb[1][:, cd * D + jc * 512: cd * D + jc * 512 + 512],
                             start=(cd == 0), stop=False)
                return e.matmul(ps, onesrow[0:1, 0:128],
                                bvrow[0:1, jc * 512:(jc + 1) * 512],
                                start=False, stop=True)
            ev_mm = P.emit("tensor", fnv, inc="t",
                           wait=[ev_WT["vp"], bank_ev[bank]])
            dst = vS[:, ti * D + jc * 512: ti * D + jc * 512 + 512]
            bank_ev[bank] = P.emit(
                "vector",
                lambda e, d=dst, p=ps: e.tensor_copy(d, p),
                inc="v", wait=[ev_mm])
    ev_p1_s = P.ev("s")
    ev_p1_v = P.ev("v")
    DBG["qT"] = (qT, [128, ND * T], BF16, [P.ev("s"), P.ev("v")])
    DBG["wkT"] = (wkT, [128, ND * T], BF16, [ev_wk_done])
    DBG["v"] = (vS, [128, NT * D], BF16, [ev_p1_s, ev_p1_v])
    DBG["xTd"] = (xT, [128, ND * T], BF16, [ev_xT])
    if stop_after == "p1":
        return finish()

    for tc in range(NTC):
        for sc in range(4 * tc + 4):
            bank = slot(tc, sc) % 4
            ps = pspr[:, bank * 512:(bank + 1) * 512]

            def fns(e, tc=tc, sc=sc, ps=ps):
                inst = None
                for cd in range(ND):
                    inst = e.matmul(ps,
                                    wkT[:, cd * T + sc * 128: cd * T + sc * 128 + 128],
                                    qT[:, cd * T + tc * 512: cd * T + tc * 512 + 512],
                                    start=(cd == 0), stop=(cd == ND - 1))
                return inst
            ev_mm = P.emit("tensor", fns, inc="t",
                           wait=[ev_wk_done, ev_p1_v, bank_ev[bank]])
            dlt = sc - 4 * tc
            dst = scoresT[:, slot(tc, sc) * 512:(slot(tc, sc) + 1) * 512]
            if dlt >= 0:
                bank_ev[bank] = P.emit(
                    "vector",
                    lambda e, d=dst, p=ps, m=dlt: e.tensor_tensor(
                        d, p, masks[:, m * 512:(m + 1) * 512], ALU.mult),
                    inc="v", wait=[ev_mm])
            else:
                bank_ev[bank] = P.emit(
                    "scalar",
                    lambda e, d=dst, p=ps: e.activation(out=d, in_=p, func=AF.Copy),
                    inc="s", wait=[ev_mm])
    ev_sc_v = P.ev("v")
    ev_sc_s = P.ev("s")
    DBG["scoresT"] = (scoresT, [128, NSLOT * 512], BF16, [ev_sc_v, ev_sc_s])
    if stop_after == "p2a":
        return finish()

    ev_xres = [None] * (NT + 2)
    ev_y = [None] * NT
    ev_hbuf_used = None
    ev_numev = [[None, None], [None, None]]
    ev_den = [None, None]
    ev_sq = None
    ev_xres_used = [None, None]
    ev_b1m = P.emit("vector", lambda e: e.memset(b1b[:], 0.0), inc="v", wait=[])
    ev_b2m = P.emit("vector", lambda e: e.memset(b2b[:], 0.0), inc="v", wait=[])
    ev_b1 = ld("gpsimd", b1b[0:8, :], b1_h, "g", wait=[ev_b1m])
    ev_b2 = ld("gpsimd", b2b[0:8, :], b2_h, "g", wait=[ev_b2m])
    ev_2a_t = P.ev("t")
    for ti in range(2):
        ev_xres[ti] = ld("sync", xres[:, (ti % 2) * D:(ti % 2 + 1) * D],
                         x_h[ti * 128:(ti + 1) * 128, :], "q", wait=[ev_2a_t])
    ev_pool_mm = None
    pool_pending = []

    def emit_pool(ti):
        def fpool(e, ti=ti):
            e.matmul(pspr[0:1, 1024:1536], onescol[:, 0:1],
                     y_acc[:, ti * D: ti * D + 512],
                     start=(ti == 0), stop=(ti == NT - 1))
            return e.matmul(pspr[0:1, 1536:2048], onescol[:, 0:1],
                            y_acc[:, ti * D + 512: ti * D + 1024],
                            start=(ti == 0), stop=(ti == NT - 1))
        return P.emit("tensor", fpool, inc="t", wait=[ev_y[ti]])

    for ti in range(NT):
        tc = ti // 4
        par = ti % 2
        pnum = psnum if par == 0 else psx
        ls = lnsc if par == 0 else lnsc2
        pden = pspr[:, 0:1] if par == 0 else pspr[:, 512:513]

        def fnum(e, ti=ti, tc=tc, jc=0, pnum=pnum):
            inst = None
            for sc in range(ti + 1):
                inst = e.matmul(pnum[:, jc * 512:(jc + 1) * 512],
                                scoresT[:, slot(tc, sc) * 512 + (ti % 4) * 128:
                                        slot(tc, sc) * 512 + (ti % 4) * 128 + 128],
                                vS[:, sc * D + jc * 512: sc * D + jc * 512 + 512],
                                start=(sc == 0), stop=(sc == ti))
            return inst
        w0 = [ev_sc_v, ev_sc_s] if ti == 0 else []
        ev_n0 = P.emit("tensor", lambda e, f=fnum, ls=ls: f(e, jc=0),
                       inc="t", wait=w0 + [ev_numev[par][0]])
        ev_n1 = P.emit("tensor", lambda e, f=fnum, ls=ls: f(e, jc=1),
                       inc="t", wait=[ev_numev[par][1]])

        def fden(e, ti=ti, tc=tc, pden=pden):
            inst = None
            for sc in range(ti + 1):
                inst = e.matmul(pden,
                                scoresT[:, slot(tc, sc) * 512 + (ti % 4) * 128:
                                        slot(tc, sc) * 512 + (ti % 4) * 128 + 128],
                                onescol_bf[:, 0:1],
                                start=(sc == 0), stop=(sc == ti))
            return inst
        ev_dn = P.emit("tensor", fden, inc="t", wait=[ev_den[par], ev_consts_g])
        if len(pool_pending) > 2:
            ev_pool_mm = emit_pool(pool_pending.pop(0))
        ev_rcp_i = P.emit("vector",
                          lambda e, pden=pden, ls=ls: e.tensor_scalar_add(
                              ls[:, 0:1], pden, cst[:, 0:1]),
                          inc="v", wait=[ev_dn])
        ev_den[par] = ev_rcp_i
        ev_rcp = P.emit("vector",
                        lambda e, ls=ls: e.reciprocal(ls[:, 1:2], ls[:, 0:1]),
                        inc="v", wait=[ev_rcp_i])
        xs = xres[:, (ti % 2) * D:(ti % 2 + 1) * D]
        ev_h0 = P.emit("vector",
                       lambda e, xs=xs, pnum=pnum, ls=ls: e.scalar_tensor_tensor(
                           out=hbuf[:, 0:512], in0=pnum[:, 0:512], scalar=ls[:, 1:2],
                           in1=xs[:, 0:512], op0=ALU.mult, op1=ALU.add),
                       inc="v", wait=[ev_n0, ev_rcp, ev_xres[ti], ev_hbuf_used])
        ev_numev[par][0] = ev_h0
        ev_h1 = P.emit("vector",
                       lambda e, xs=xs, pnum=pnum, ls=ls: e.scalar_tensor_tensor(
                           out=hbuf[:, 512:1024], in0=pnum[:, 512:1024],
                           scalar=ls[:, 1:2],
                           in1=xs[:, 512:1024], op0=ALU.mult, op1=ALU.add),
                       inc="v", wait=[ev_n1])
        ev_numev[par][1] = ev_h1
        ev_xres_used[ti % 2] = ev_h1
        if ti + 2 < NT:
            ev_xres[ti + 2] = ld("sync", xres[:, ((ti + 2) % 2) * D:((ti + 2) % 2 + 1) * D],
                                 x_h[(ti + 2) * 128:(ti + 3) * 128, :], "q",
                                 wait=[ev_xres_used[ti % 2]])
        ev_sum = P.emit("vector",
                        lambda e, ls=ls: e.reduce_sum(ls[:, 2:3], hbuf[:], axis=AX.X),
                        inc="v", wait=[ev_h1])
        ev_sq = P.emit("scalar",
                       lambda e, ls=ls: e.activation(out=sqd[:], in_=hbuf[:], func=AF.Square,
                                              accum_out=ls[:, 3:4]),
                       inc="s", wait=[ev_h1, ev_sq])
        ev_m = P.emit("vector",
                      lambda e, ls=ls: e.tensor_scalar_mul(ls[:, 4:5], ls[:, 2:3], 1.0 / D),
                      inc="v", wait=[ev_sum])
        ev_v1 = P.emit("vector",
                       lambda e, ls=ls: e.tensor_scalar_mul(ls[:, 5:6], ls[:, 3:4], 1.0 / D),
                       inc="v", wait=[ev_sq])
        ev_v2 = P.emit("vector",
                       lambda e, ls=ls: e.tensor_mul(ls[:, 6:7], ls[:, 4:5], ls[:, 4:5]),
                       inc="v", wait=[ev_m])
        ev_v3 = P.emit("vector",
                       lambda e, ls=ls: e.tensor_sub(ls[:, 7:8], ls[:, 5:6], ls[:, 6:7]),
                       inc="v", wait=[ev_v1, ev_v2])
        ev_l = P.emit("scalar",
                      lambda e, ls=ls: e.activation(out=ls[:, 5:6], in_=ls[:, 7:8],
                                             func=AF.Ln, bias=cst[:, 1:2]),
                      inc="s", wait=[ev_v3])
        ev_r = P.emit("scalar",
                      lambda e, ls=ls: e.activation(out=ls[:, 6:7], in_=ls[:, 5:6],
                                             func=AF.Exp, scale=cst[:, 5:6]),
                      inc="s", wait=[ev_l])
        ev_y[ti] = P.emit("vector",
                          lambda e, ti=ti, ls=ls: e.tensor_scalar(
                              y_acc[:, ti * D:(ti + 1) * D], hbuf[:],
                              ls[:, 4:5], ls[:, 6:7],
                              op0=ALU.subtract, op1=ALU.mult),
                          inc="v", wait=[ev_r])
        ev_hbuf_used = ev_y[ti]
        pool_pending.append(ti)
        if stop_after == "t0" and ti == 0:
            DBG["hbuf"] = (hbuf, [128, D], F32, [ev_y[0]])
            DBG["lnsc"] = (lnsc, [128, 8], F32, [ev_y[0]])
            DBG["xres0"] = (xres, [128, 2 * D], F32, [ev_y[0]])
            DBG["y1t0"] = (y_acc, [128, NT * D], F32, [ev_y[0]])
            return finish()

    while pool_pending:
        ev_pool_mm = emit_pool(pool_pending.pop(0))

    DBG["y1"] = (y_acc, [128, NT * D], F32, [ev_y[NT - 1]])
    if stop_after == "p2b":
        return finish()

    ev_pooled = P.emit("vector",
                       lambda e: e.tensor_scalar_mul(outst[0:1, 0:1024],
                                                     pspr[0:1, 1024:2048], 1.0 / T),
                       inc="v", wait=[ev_pool_mm])

    def ftrp(e):
        inst = None
        for c in range(ND):
            inst = e.matmul(pspr[:, 1024 + c: 1025 + c],
                            outst[0:1, c * 128:(c + 1) * 128],
                            onesrow[0:1, 0:1],
                            start=(c == 0), stop=(c == ND - 1))
        return inst
    ev_poolT = P.emit("tensor", ftrp, inc="t", wait=[ev_pooled])
    ev_poolT_sb = P.emit("vector",
                         lambda e: e.tensor_copy(rcol[:, 8:16], pspr[:, 1024:1032]),
                         inc="v", wait=[ev_poolT])

    def flog(e):
        inst = None
        for c in range(ND):
            inst = e.matmul(pspr[0:1, 1536:1544], rcol[:, 8 + c: 9 + c],
                            Wr[:, c * 8:(c + 1) * 8],
                            start=(c == 0), stop=(c == ND - 1))
        return inst
    ev_logmm = P.emit("tensor", flog, inc="t", wait=[ev_poolT_sb])
    ev_log = P.emit("vector", lambda e: e.tensor_copy(rsc[0:1, 0:8], pspr[0:1, 1536:1544]),
                    inc="v", wait=[ev_logmm])
    e1 = P.emit("vector", lambda e: e.reduce_max(rsc[0:1, 8:9], rsc[0:1, 0:8], axis=AX.X),
                inc="v", wait=[ev_log])
    e2_ = P.emit("vector", lambda e: e.tensor_scalar(rmsk[0:1, :], rsc[0:1, 0:8],
                                                     rsc[0:1, 8:9], None, op0=ALU.is_equal),
                 inc="v", wait=[e1])
    e3 = P.emit("vector", lambda e: e.select(rsc[0:1, 24:32], rmsk[0:1, :],
                                             i8r[0:1, :], bigr[0:1, :]),
                inc="v", wait=[e2_])
    e4 = P.emit("vector", lambda e: e.tensor_reduce(rsc[0:1, 9:10], rsc[0:1, 24:32],
                                                    axis=AX.X, op=ALU.min),
                inc="v", wait=[e3])
    e5 = P.emit("vector", lambda e: e.tensor_scalar(rsc[0:1, 16:24], i8r[0:1, :],
                                                    rsc[0:1, 9:10], None, op0=ALU.is_equal),
                inc="v", wait=[e4])
    e6 = P.emit("vector", lambda e: e.tensor_scalar(rsc[0:1, 32:40], rsc[0:1, 16:24],
                                                    -1e30, None, op0=ALU.mult),
                inc="v", wait=[e5])
    e7 = P.emit("vector", lambda e: e.tensor_add(rsc[0:1, 32:40], rsc[0:1, 32:40],
                                                 rsc[0:1, 0:8]),
                inc="v", wait=[e6])
    e8 = P.emit("vector", lambda e: e.reduce_max(rsc[0:1, 10:11], rsc[0:1, 32:40],
                                                 axis=AX.X),
                inc="v", wait=[e7])
    e9 = P.emit("vector", lambda e: e.tensor_scalar(rmsk[0:1, :], rsc[0:1, 32:40],
                                                    rsc[0:1, 10:11], None,
                                                    op0=ALU.is_equal),
                inc="v", wait=[e8])
    e10 = P.emit("vector", lambda e: e.select(rsc[0:1, 24:32], rmsk[0:1, :],
                                              i8r[0:1, :], bigr[0:1, :]),
                 inc="v", wait=[e9])
    e11 = P.emit("vector", lambda e: e.tensor_reduce(rsc[0:1, 11:12], rsc[0:1, 24:32],
                                                     axis=AX.X, op=ALU.min),
                 inc="v", wait=[e10])
    e12 = P.emit("vector", lambda e: e.tensor_sub(rsc[0:1, 40:41], rsc[0:1, 10:11],
                                                  rsc[0:1, 8:9]),
                 inc="v", wait=[e11])
    e13 = P.emit("scalar", lambda e: e.activation(out=rsc[0:1, 12:13],
                                                  in_=rsc[0:1, 40:41], func=AF.Exp),
                 inc="s", wait=[e12])
    e14 = P.emit("vector", lambda e: e.tensor_scalar_add(rsc[0:1, 13:14],
                                                         rsc[0:1, 12:13], 1.0),
                 inc="v", wait=[e13])
    e15 = P.emit("vector", lambda e: e.reciprocal(rsc[0:1, 14:15], rsc[0:1, 13:14]),
                 inc="v", wait=[e14])
    e16 = P.emit("vector", lambda e: e.tensor_mul(rsc[0:1, 15:16], rsc[0:1, 12:13],
                                                  rsc[0:1, 14:15]),
                 inc="v", wait=[e15])

    ev_trb = [ev_numev[0][1], ev_numev[1][1]]
    ev_x1b_used = None
    for ti in range(NT):
        ev_cp = P.emit("scalar",
                       lambda e, ti=ti: e.activation(
                           out=x1btmp[:], in_=y_acc[:, ti * D:(ti + 1) * D],
                           func=AF.Copy),
                       inc="s", wait=[ev_y[ti], ev_x1b_used])
        pb = ti % 2
        pst = psnum if pb == 0 else psx

        def ftr(e, pst=pst):
            inst = None
            for c in range(ND):
                inst = e.matmul(pst[:, c * 128:(c + 1) * 128],
                                x1btmp[:, c * 128:(c + 1) * 128], ident_sb[:],
                                start=(c in (0, 4)), stop=(c in (3, 7)))
            return inst
        ev_tr = P.emit("tensor", ftr, inc="t", wait=[ev_cp, ev_ident, ev_trb[pb]])
        dst = x1T.rearrange("p (c t) -> p c t", c=ND)[:, :, ti * 128:(ti + 1) * 128]
        src_ = pst.rearrange("p (c t) -> p c t", c=ND)
        ev_x1b_used = ev_tr
        ev_trb[pb] = P.emit("vector",
                            lambda e, d=dst, s=src_: e.tensor_copy(d, s),
                            inc="v", wait=[ev_tr])
    ev_x1T_all_v = P.ev("v")

    def fbc(e):
        srcs = [rsc[0:1, 9:10], rsc[0:1, 11:12], rsc[0:1, 14:15], rsc[0:1, 15:16]]
        inst = None
        for j, s in enumerate(srcs):
            inst = e.matmul(pspr[:, j:j + 1], onesrow[0:1, 0:128], s,
                            start=(j == 0), stop=(j == 3))
        return inst
    ev_bc = P.emit("tensor", fbc, inc="t", wait=[e16, ev_den[0], ev_den[1]])
    ev_bc_sb = P.emit("vector", lambda e: e.tensor_copy(rcol[:, 0:4], pspr[:, 0:4]),
                      inc="v", wait=[ev_bc])

    def fidx(e):
        e.tensor_scalar(rcol[:, 4:5], rcol[:, 0:1], 1024.0, None, op0=ALU.mult)
        e.tensor_scalar(rcol[:, 5:6], rcol[:, 1:2], 1024.0, None, op0=ALU.mult)
        e.tensor_scalar(rcol[:, 6:7], rcol[:, 0:1], 4096.0, None, op0=ALU.mult)
        return e.tensor_scalar(rcol[:, 7:8], rcol[:, 1:2], 4096.0, None, op0=ALU.mult)
    ev_idx0 = P.emit("vector", fidx, inc="v", wait=[ev_bc_sb])

    def fidx2(e):
        inst = None
        for k in range(2):
            e.tensor_tensor(ridx1[:, k * 8:(k + 1) * 8],
                            rcol[:, 4 + k:5 + k].to_broadcast([128, 8]), iw1[:], ALU.add)
        for k in range(2):
            inst = e.tensor_tensor(ridx2[:, k * 32:(k + 1) * 32],
                                   rcol[:, 6 + k:7 + k].to_broadcast([128, 32]), iw2[:],
                                   ALU.add)
        return inst
    ev_idx = P.emit("vector", fidx2, inc="v", wait=[ev_idx0])

    ev_gath_w1 = [None] * 8

    def gather_block(blk, waits):
        k, qq = blk // 4, blk % 4
        bp = blk % 2
        evs = []
        for cd in range(ND):
            def fg(e, k=k, qq=qq, bp=bp, cd=cd):
                return e.indirect_dma_start(
                    out=W1q[bp][:, cd * 1024:(cd + 1) * 1024],
                    out_offset=None,
                    in_=W1flat,
                    in_offset=IndirectOffsetOnAxis(ridx1[:, k * 8 + cd: k * 8 + cd + 1], 0),
                    element_offset=qq * 1024)
            evs.append(P.emit("gpsimd", fg, inc="g", dma=True, wait=waits))
            waits = []
        ev_gath_w1[blk] = evs[-1]
        for fc in range(8):
            def fg2(e, k=k, qq=qq, bp=bp, fc=fc):
                return e.indirect_dma_start(
                    out=W2q[bp][:, fc * 1024:(fc + 1) * 1024],
                    out_offset=None,
                    in_=W2flat,
                    in_offset=IndirectOffsetOnAxis(
                        ridx2[:, k * 32 + qq * 8 + fc: k * 32 + qq * 8 + fc + 1], 0),
                    element_offset=0)
            evs.append(P.emit("gpsimd", fg2, inc="g", dma=True, wait=[]))
        return evs[-1]

    ev_gath = [None] * 8
    ev_blk_t = [None] * 8
    ev_gath[0] = gather_block(0, [ev_idx])
    ev_gath[1] = gather_block(1, [])

    ev_ohm = P.emit("vector", lambda e: e.memset(oh8[:], 0.0), inc="v", wait=[])
    ev_oh0 = P.emit("vector",
                    lambda e: e.tensor_scalar(oh8[0:8, 0:1], icol[0:8, :],
                                              rcol[0:8, 0:1], None, op0=ALU.is_equal),
                    inc="v", wait=[ev_bc_sb, ev_ohm])
    ev_oh1 = P.emit("vector",
                    lambda e: e.tensor_scalar(oh8[0:8, 1:2], icol[0:8, :],
                                              rcol[0:8, 1:2], None, op0=ALU.is_equal),
                    inc="v", wait=[ev_bc_sb, ev_ohm])
    ev_g8a = P.emit("vector",
                    lambda e: e.tensor_scalar(oh8[0:8, 2:3], oh8[0:8, 0:1],
                                              rcol[0:8, 2:3], None, op0=ALU.mult),
                    inc="v", wait=[ev_oh0])
    ev_g8b = P.emit("vector",
                    lambda e: e.scalar_tensor_tensor(
                        out=oh8[0:8, 3:4], in0=oh8[0:8, 1:2], scalar=rcol[0:8, 3:4],
                        in1=oh8[0:8, 2:3], op0=ALU.mult, op1=ALU.add),
                    inc="v", wait=[ev_oh1, ev_g8a])
    def fb2(e):
        e.matmul(psnum[0:1, 0:512], oh8[:, 3:4], b2b[:, 0:512], start=True, stop=True)
        return e.matmul(psnum[0:1, 512:1024], oh8[:, 3:4], b2b[:, 512:1024],
                        start=True, stop=True)
    ev_b2mm = P.emit("tensor", fb2, inc="t", wait=[ev_g8b, ev_b2, ev_trb[0]])
    ev_b2row = P.emit("vector",
                      lambda e: e.tensor_copy(b2row[0:1, :], psnum[0:1, 0:1024]),
                      inc="v", wait=[ev_b2mm])

    def fb1(e):
        inst = None
        for k in range(2):
            for fc in range(32):
                inst = e.matmul(psx[:, k * 32 + fc: k * 32 + fc + 1],
                                b1b[:, fc * 128:(fc + 1) * 128], oh8[:, k:k + 1],
                                start=(k == 0 and fc == 0), stop=(k == 1 and fc == 31))
        return inst
    ev_b1mm = P.emit("tensor", fb1, inc="t", wait=[ev_oh1, ev_b1, ev_trb[1]])
    ev_b1sel = P.emit("vector",
                      lambda e: e.tensor_copy(b1selT[:], psx[:, 0:64]),
                      inc="v", wait=[ev_b1mm])

    DBG["router"] = (rsc, [1, 64], F32, [e16, e4, e11])
    DBG["rcol"] = (rcol, [128, 16], F32, [ev_idx])
    DBG["x1T"] = (x1T, [128, ND * T], BF16, [ev_x1T_all_v])
    DBG["W1qA"] = (W1q[0], [128, ND * 1024], BF16, [ev_gath[0]])
    DBG["W2qA"] = (W2q[0], [128, 8 * D], BF16, [ev_gath[0]])
    DBG["b1selT"] = (b1selT, [128, 64], F32, [ev_b1sel])
    DBG["b2row"] = (b2row, [1, D], BF16, [ev_b2row])
    if stop_after == "pr":
        return finish()

    ev_hsb_used = [None, None]
    ev_hps_free = [ev_b2row, ev_b1sel]
    ev_ylast = ev_bc_sb
    ev_yv = [None] * 8
    for blk in range(8):
        k, qq = blk // 4, blk % 4
        bp = blk % 2
        for tb in range(8):
            for g in range(4):
                par = g % 2
                pst = psnum if par == 0 else psx
                hs = hsb[par]

                def fA(e, k=k, qq=qq, bp=bp, tb=tb, g=g, pst=pst):
                    inst = None
                    for fcl in range(2):
                        fc = g * 2 + fcl
                        od = pst[:, fcl * 512: fcl * 512 + 256]
                        for cd in range(ND):
                            inst = e.matmul(
                                od,
                                W1q[bp][:, cd * 1024 + fc * 128: cd * 1024 + fc * 128 + 128],
                                x1T[:, cd * T + tb * 256: cd * T + tb * 256 + 256],
                                start=(cd == 0), stop=(cd == ND - 1))
                    return inst
                wA = [ev_gath_w1[blk] if g == 0 else None, ev_hps_free[par], ev_b1sel]
                if blk == 0 and tb == 0 and g == 0:
                    wA.append(ev_x1T_all_v)
                ev_A = P.emit("tensor", fA, inc="t", wait=wA)
                bcol0 = k * 32 + qq * 8 + g * 2
                P.emit("scalar",
                       lambda e, hs=hs, pst=pst, bc=bcol0: e.activation(
                           out=hs[:, 0:256], in_=pst[:, 0:256], func=AF.Gelu,
                           bias=b1selT[:, bc:bc + 1]),
                       inc="s", wait=[ev_A, ev_hsb_used[par]])
                ev_gelu = P.emit(
                    "scalar",
                    lambda e, hs=hs, pst=pst, bc=bcol0: e.activation(
                        out=hs[:, 256:512], in_=pst[:, 512:768], func=AF.Gelu,
                        bias=b1selT[:, bc + 1:bc + 2]),
                    inc="s", wait=[])
                ev_hps_free[par] = ev_gelu

                def fB(e, k=k, qq=qq, bp=bp, tb=tb, g=g, blk=blk, hs=hs):
                    inst = None
                    for fcl in range(2):
                        fc = g * 2 + fcl
                        first = (g == 0 and fcl == 0)
                        last = (g == 3 and fcl == 1)
                        for tl in range(2):
                            for jc in range(2):
                                od = pspr[:, (tl * 2 + jc) * 512:(tl * 2 + jc) * 512 + 512]
                                if blk == 0 and first:
                                    e.matmul(od, onesrow_bf[0:1, 0:128],
                                             b2row[0:1, jc * 512:(jc + 1) * 512],
                                             start=True, stop=False)
                                inst = e.matmul(
                                    od,
                                    hs[:, fcl * 256 + tl * 128: fcl * 256 + tl * 128 + 128],
                                    W2q[bp][:, fc * 1024 + jc * 512: fc * 1024 + jc * 512 + 512],
                                    start=(first and blk != 0), stop=last)
                    return inst
                ev_B = P.emit("tensor", fB, inc="t",
                              wait=[ev_gelu, ev_ylast if g == 0 else None,
                                    ev_gath[blk] if g == 0 else None, ev_b2row])
                ev_hsb_used[par] = ev_B
            wk_col = rcol[:, 2 + k:3 + k]

            def fy(e, tb=tb, wk_col=wk_col):
                inst = None
                for tl in range(2):
                    for jc in range(2):
                        od = pspr[:, (tl * 2 + jc) * 512:(tl * 2 + jc) * 512 + 512]
                        dst = y_acc[:, (tb * 2 + tl) * D + jc * 512:
                                    (tb * 2 + tl) * D + jc * 512 + 512]
                        inst = e.scalar_tensor_tensor(out=dst, in0=od, scalar=wk_col,
                                                      in1=dst, op0=ALU.mult, op1=ALU.add)
                return inst
            ev_yv[tb] = P.emit("vector", fy, inc="v", wait=[ev_B])
            ev_ylast = ev_yv[tb]
        ev_blk_t[blk] = P.ev("t")
        if blk + 2 < 8:
            ev_gath[blk + 2] = gather_block(blk + 2, [ev_blk_t[blk]])

    DBG["y2"] = (y_acc, [128, NT * D], F32, [ev_yv[7], P.ev("v"), P.ev("t")])
    if stop_after == "p3":
        return finish()

    ev_out_sq = [None, None]
    ev_st_used = [None, None]
    for ti in range(NT):
        tb = ti // 2
        par = ti % 2
        ls = lnsc if par == 0 else lnsc2
        sq2 = sqd2 if par == 0 else sqd3
        ev_sum2 = P.emit("vector",
                         lambda e, ti=ti, ls=ls: e.reduce_sum(
                             ls[:, 2:3], y_acc[:, ti * D:(ti + 1) * D], axis=AX.X),
                         inc="v", wait=[ev_yv[tb]])
        ev_out_sq[par] = P.emit("scalar",
                                lambda e, ti=ti, ls=ls, sq2=sq2: e.activation(
                                    out=sq2[:], in_=y_acc[:, ti * D:(ti + 1) * D],
                                    func=AF.Square, accum_out=ls[:, 3:4]),
                                inc="s", wait=[ev_yv[tb], ev_out_sq[par]])
        ev_m = P.emit("vector",
                      lambda e, ls=ls: e.tensor_scalar_mul(ls[:, 4:5], ls[:, 2:3], 1.0 / D),
                      inc="v", wait=[ev_sum2])
        ev_v1 = P.emit("vector",
                       lambda e, ls=ls: e.tensor_scalar_mul(ls[:, 5:6], ls[:, 3:4], 1.0 / D),
                       inc="v", wait=[ev_out_sq[par]])
        ev_v2 = P.emit("vector",
                       lambda e, ls=ls: e.tensor_mul(ls[:, 6:7], ls[:, 4:5], ls[:, 4:5]),
                       inc="v", wait=[ev_m])
        ev_v3 = P.emit("vector",
                       lambda e, ls=ls: e.tensor_sub(ls[:, 7:8], ls[:, 5:6], ls[:, 6:7]),
                       inc="v", wait=[ev_v1, ev_v2])
        ev_l = P.emit("scalar",
                      lambda e, ls=ls: e.activation(out=ls[:, 5:6], in_=ls[:, 7:8],
                                                    func=AF.Ln, bias=cst[:, 1:2]),
                      inc="s", wait=[ev_v3])
        ev_r = P.emit("scalar",
                      lambda e, ls=ls: e.activation(out=ls[:, 6:7], in_=ls[:, 5:6],
                                                    func=AF.Exp, scale=cst[:, 5:6]),
                      inc="s", wait=[ev_l])
        ev_o = P.emit("vector",
                      lambda e, ti=ti, par=par, ls=ls: e.tensor_scalar(
                          outst[:, par * D:(par + 1) * D],
                          y_acc[:, ti * D:(ti + 1) * D],
                          ls[:, 4:5], ls[:, 6:7],
                          op0=ALU.subtract, op1=ALU.mult),
                      inc="v", wait=[ev_r, ev_st_used[par]])
        ev_st_used[par] = ld("sync", out_h[ti * 128:(ti + 1) * 128, :],
                             outst[:, par * D:(par + 1) * D], "q", wait=[ev_o])

    return finish()


def make_consts():
    cst = np.zeros((128, 8), np.float32)
    cst[:, 0] = 1e-6
    cst[:, 1] = 1e-5
    cst[:, 2] = 1.0 / D
    cst[:, 3] = 1.0 / T
    cst[:, 4] = -1.0
    cst[:, 5] = -0.5
    cst[:, 6] = 1.0
    p = np.arange(128, dtype=np.uint32)
    iw1 = (p[:, None] + 128 * np.arange(8, dtype=np.uint32)[None, :]).astype(np.uint32)
    iw2 = np.zeros((128, 32), np.uint32)
    for qq in range(4):
        for fc in range(8):
            iw2[:, qq * 8 + fc] = qq * 1024 + fc * 128 + p
    masks = np.zeros((128, 4, 512), ml_dtypes.bfloat16)
    f = np.arange(512)
    for dlt in range(4):
        masks[:, dlt, :] = (f[None, :] - np.arange(128)[:, None] >= 128 * dlt)
    return {
        "ident": np.eye(128, dtype=ml_dtypes.bfloat16),
        "masks": masks.reshape(128, 2048),
        "cst": cst,
        "iotaW1": iw1,
        "iotaW2": iw2,
        "iota8row": np.arange(8, dtype=np.float32).reshape(1, 8),
        "iotacol": np.arange(128, dtype=np.float32).reshape(128, 1),
        "bigrow": np.full((1, 8), 1e9, np.float32),
        "onespack": np.ones((128, 4), np.float32),
        "onesrow": np.ones((1, 512), np.float32),
    }


def make_in_maps(inputs):
    consts = make_consts()
    xs = np.asarray(inputs["L0_attn_x_seq"])
    shared = dict(consts)
    for p, key in (("q", "L0_attn_W_q"), ("k", "L0_attn_W_k"),
                   ("v", "L0_attn_W_v"), ("a", "L0_attn_W_a")):
        shared[f"W{p}T"] = np.ascontiguousarray(np.asarray(inputs[key]).T)
    for p, key in (("q", "L0_attn_b_q"), ("k", "L0_attn_b_k"), ("a", "L0_attn_b_a")):
        shared[f"b{p}128"] = np.ascontiguousarray(
            np.asarray(inputs[key]).reshape(8, 128).T)
    shared["bvrow"] = np.asarray(inputs["L0_attn_b_v"]).reshape(1, D)
    shared["Wr128"] = np.ascontiguousarray(
        np.asarray(inputs["L0_ffn_Wr"]).reshape(8, 128, 8).transpose(1, 0, 2).reshape(128, 64))
    shared["b1"] = np.asarray(inputs["L0_ffn_b1"])
    shared["b2"] = np.asarray(inputs["L0_ffn_b2"])
    shared["W1"] = np.asarray(inputs["L0_ffn_W1"])
    shared["W2"] = np.asarray(inputs["L0_ffn_W2"])
    in_maps = []
    for b in range(8):
        m = dict(shared)
        m["x"] = np.ascontiguousarray(xs[b])
        m["xT"] = np.ascontiguousarray(xs[b].T)
        in_maps.append(m)
    return in_maps


def kernel(**inputs):
    nc = build_nc()
    in_maps = make_in_maps(inputs)
    res = run_bass_kernel_spmd(nc, in_maps, core_ids=list(range(8)))
    out = np.stack([np.asarray(r["out"]) for r in res.results])
    return out.astype(np.float32)


# version 08
# speedup vs baseline: 1.3067x; 1.3067x over previous
import numpy as np
import ml_dtypes
import concourse.bass as bass
import concourse.mybir as mybir
from concourse.bass_utils import run_bass_kernel_spmd
from concourse.bass_core_rust import IndirectOffsetOnAxis

F32 = mybir.dt.float32
BF16 = mybir.dt.bfloat16
U32 = mybir.dt.uint32
AF = mybir.ActivationFunctionType
ALU = mybir.AluOpType
AX = mybir.AxisListType

T, D, FF, E = 2048, 1024, 4096, 8
NT = T // 128
ND = D // 128
NTC = 4
NSLOT = 40
SBUF_CAP = 207000

ENGINES = ("sync", "gpsimd", "tensor", "vector", "scalar")


def slot(tc, sc):
    return 2 * tc * tc + 2 * tc + sc


class Prog:
    def __init__(self, nc):
        self.nc = nc
        self.ops = {e: [] for e in ENGINES}
        self.sem = {}
        self.cnt = {}
        for name in ("q", "g", "t", "v", "s"):
            self.sem[name] = nc.alloc_semaphore("sem_" + name)
            self.cnt[name] = 0

    def emit(self, eng, fn=None, inc=None, wait=(), dma=False):
        w = []
        for ev in wait:
            if ev is None:
                continue
            sname, val = ev
            assert val <= self.cnt[sname], (sname, val, self.cnt[sname])
            if val > 0:
                w.append((self.sem[sname], val))
        ev_out = None
        n = 16 if dma else 1
        if inc is not None:
            self.cnt[inc] += n
            ev_out = (inc, self.cnt[inc])
        self.ops[eng].append((w, fn, self.sem[inc] if inc else None, n))
        return ev_out

    def ev(self, sname):
        return (sname, self.cnt[sname])

    def into_block(self, block):
        def mk(eng_name):
            def body(eng):
                for (waits, fn, sem, n) in self.ops[eng_name]:
                    for (s, v) in waits:
                        eng.wait_ge(s, v)
                    inst = fn(eng) if fn is not None else None
                    if sem is not None:
                        assert inst is not None, eng_name
                        inst.then_inc(sem, n)
            return body
        for e in ENGINES:
            getattr(block, e)(mk(e))


class Alloc:
    def __init__(self, nc, cap=None):
        self.nc = nc
        self.base = (nc.sbuf_base + 63) // 64 * 64
        self.cap = cap if cap is not None else nc.sbuf_top
        self.items = []

    def at(self, name, shape, dtype, ph0, ph1, align=64):
        nb = int(np.prod(shape[1:])) * mybir.dt.size(dtype)
        occ = sorted((s, e) for (s, e, p0, p1, _) in self.items
                     if not (p1 < ph0 or p0 > ph1))
        pos = self.base
        for (s, e) in occ:
            if pos + nb <= s:
                break
            pos = max(pos, e)
            pos = (pos + align - 1) // align * align
        assert pos + nb <= self.cap, (name, pos, nb)
        self.items.append((pos, pos + nb, ph0, ph1, name))
        return self.nc.alloc_sbuf_tensor_at(name, list(shape), dtype, offset=pos).ap()


def build_nc(debug_outputs=(), stop_after=None):
    nc = bass.Bass("TRN2", target_bir_lowering=False)

    xT_h = nc.dram_tensor("xT", [D, T], F32, kind="ExternalInput").ap()
    x_h = nc.dram_tensor("x", [T, D], F32, kind="ExternalInput").ap()
    WT_h = {p: nc.dram_tensor(f"W{p}T", [D, D], F32, kind="ExternalInput").ap()
            for p in ("q", "k", "v", "a")}
    b128_h = {p: nc.dram_tensor(f"b{p}128", [128, 8], F32, kind="ExternalInput").ap()
              for p in ("q", "k", "a")}
    bv_h = nc.dram_tensor("bvrow", [1, D], F32, kind="ExternalInput").ap()
    Wr_h = nc.dram_tensor("Wr128", [128, 64], F32, kind="ExternalInput").ap()
    b1_h = nc.dram_tensor("b1", [E, FF], F32, kind="ExternalInput").ap()
    b2_h = nc.dram_tensor("b2", [E, D], F32, kind="ExternalInput").ap()
    W1_h = nc.dram_tensor("W1", [E, D, FF], F32, kind="ExternalInput").ap()
    W2_h = nc.dram_tensor("W2", [E, FF, D], F32, kind="ExternalInput").ap()
    W1flat = W1_h.rearrange("e d f -> (e d) f")
    W2flat = W2_h.rearrange("e f d -> (e f) d")
    ident_h = nc.dram_tensor("ident", [128, 128], BF16, kind="ExternalInput").ap()
    masks_h = nc.dram_tensor("masks", [128, 2048], BF16, kind="ExternalInput").ap()
    cst_h = nc.dram_tensor("cst", [128, 8], F32, kind="ExternalInput").ap()
    iw1_h = nc.dram_tensor("iotaW1", [128, 8], U32, kind="ExternalInput").ap()
    iw2_h = nc.dram_tensor("iotaW2", [128, 32], U32, kind="ExternalInput").ap()
    i8r_h = nc.dram_tensor("iota8row", [1, 8], F32, kind="ExternalInput").ap()
    icol_h = nc.dram_tensor("iotacol", [128, 1], F32, kind="ExternalInput").ap()
    big_h = nc.dram_tensor("bigrow", [1, 8], F32, kind="ExternalInput").ap()
    ones_h = nc.dram_tensor("onespack", [128, 4], F32, kind="ExternalInput").ap()
    onesrow_h = nc.dram_tensor("onesrow", [1, 512], F32, kind="ExternalInput").ap()

    out_h = nc.dram_tensor("out", [T, D], F32, kind="ExternalOutput").ap()

    al = Alloc(nc)
    C = lambda name, shape, dt: al.at(name, shape, dt, 0, 99)
    masks = C("c_masks", [128, 2048], BF16)
    cst = C("c_cst", [128, 8], F32)
    iw1 = C("c_iw1", [128, 8], U32)
    iw2 = C("c_iw2", [128, 32], U32)
    i8r = C("c_i8r", [1, 8], F32)
    icol = C("c_icol", [128, 1], F32)
    bigr = C("c_big", [1, 8], F32)
    onescol = C("c_onescol", [128, 4], F32)
    onescol_bf = C("c_onescol_bf", [128, 4], BF16)
    onesrow = C("c_onesrow", [1, 512], F32)
    onesrow_bf = C("c_onesrow_bf", [1, 512], BF16)
    bq = C("c_bq", [128, 8], F32)
    bk = C("c_bk", [128, 8], F32)
    ba = C("c_ba", [128, 8], F32)
    Wr = C("c_wr", [128, 64], F32)
    rsc = C("r_sc", [1, 64], F32)
    rcol = C("r_col", [128, 16], F32)
    ridx1 = C("r_idx1", [128, 16], U32)
    ridx2 = C("r_idx2", [128, 64], U32)
    oh8 = C("r_oh8", [128, 4], BF16)
    lnsc = C("r_lnsc", [128, 8], F32)
    rmsk = C("r_msk", [1, 8], U32)
    lnsc2 = C("r_lnsc2", [128, 8], F32)
    lnS = C("r_lnS", [128, 16], F32)
    lnQ = C("r_lnQ", [128, 16], F32)
    lnM = C("r_lnM", [128, 16], F32)
    lnV = C("r_lnV", [128, 16], F32)
    ident_sb = C("c_ident", [128, 128], BF16)

    xT = al.at("xT", [128, ND * T], BF16, 1, 1)
    qT = al.at("qT", [128, ND * T], BF16, 1, 2)
    y_acc = al.at("y_acc", [128, NT * D], F32, 3, 6)
    WTb = [al.at(f"WTb{i}", [128, ND * D], BF16, 1, 1) for i in range(2)]
    WTq = al.at("WTq", [128, ND * D], BF16, 1, 1)
    scoresT = al.at("scoresT", [128, NSLOT * 512], BF16, 2, 3)
    x1T = al.at("x1T", [128, ND * T], BF16, 4, 6)
    b1b = al.at("b1b", [128, FF], BF16, 3, 5)
    wkT = al.at("wkT", [128, ND * T], BF16, 1, 2)
    W1q = [None, None]
    W2q = [None, None]
    W1q[0] = al.at("W1qA", [128, ND * 1024], BF16, 4, 6)
    W2q[0] = al.at("W2qA", [128, 8 * D], BF16, 4, 6)
    vS = al.at("v", [128, NT * D], BF16, 1, 3)
    W1q[1] = al.at("W1qB", [128, ND * 1024], BF16, 4, 6)
    W2q[1] = al.at("W2qB", [128, 8 * D], BF16, 4, 6)
    Abuf = al.at("Abuf", [128, T], F32, 1, 1)
    Gbuf = al.at("Gbuf", [128, T], BF16, 1, 1)
    bvrow = al.at("bvrow", [1, D], F32, 1, 1)
    hbuf = al.at("hbuf", [128, D], F32, 3, 3)
    xres = al.at("xres", [128, 2 * D], F32, 3, 3)
    sqd = al.at("sqd", [128, D], F32, 3, 3)
    hsb = [al.at("hsbA", [128, 512], BF16, 4, 5),
           al.at("hsbB", [128, 512], BF16, 4, 5)]
    x1btmp = al.at("x1btmp", [128, D], BF16, 4, 4)
    b2b = al.at("b2b", [128, D], BF16, 3, 5)
    b2row = al.at("b2row", [1, D], BF16, 4, 5)
    b1selT = al.at("b1selT", [128, 64], F32, 4, 5)
    outst = al.at("outst", [128, 2 * D], F32, 4, 6)
    sqd2 = al.at("sqd2", [128, D], F32, 5, 6)

    pspr = nc.alloc_psum_tensor("pspr", [128, 2048], F32).ap()
    psnum = nc.alloc_psum_tensor("psnum", [128, 1024], F32).ap()
    psx = nc.alloc_psum_tensor("psx", [128, 1024], F32).ap()

    P = Prog(nc)
    DBG = {}

    def finish():
        for name in debug_outputs:
            ap, shape, dt, waits = DBG[name]
            dh = nc.dram_tensor("dbg_" + name, list(shape), dt, kind="ExternalOutput").ap()
            P.emit("sync", lambda e, o=dh, i=ap: e.dma_start(out=o, in_=i),
                   inc="q", dma=True, wait=waits)
        P.emit("sync", None, wait=[P.ev("q"), P.ev("g")])
        with nc.Block() as block:
            P.into_block(block)
        return nc

    def ld(engq, out_ap, in_ap, semname, wait=()):
        return P.emit(engq, lambda e, o=out_ap, i=in_ap: e.dma_start(out=o, in_=i),
                      inc=semname, dma=True, wait=wait)

    for o, i in ((masks, masks_h), (cst, cst_h), (iw1, iw1_h),
                 (iw2, iw2_h), (i8r, i8r_h), (icol, icol_h), (bigr, big_h),
                 (onescol, ones_h), (onesrow, onesrow_h),
                 (bq, b128_h["q"]), (bk, b128_h["k"]), (ba, b128_h["a"]),
                 (Wr, Wr_h), (bvrow, bv_h)):
        ld("sync", o, i, "q")
    ev_ident = ld("sync", ident_sb, ident_h, "q")
    ld("gpsimd", onescol_bf, ones_h, "g")
    ld("gpsimd", onesrow_bf, onesrow_h, "g")
    ev_consts_q = P.ev("q")
    ev_WT = {}
    ev_WT["a"] = ld("gpsimd", WTb[0].rearrange("p (c i) -> p c i", c=ND),
                    WT_h["a"].rearrange("(c p) i -> p c i", p=128), "g")
    ev_xTs = [None] * NTC
    xT3 = xT.rearrange("p (c t) -> p c t", c=ND)
    xTh3 = xT_h.rearrange("(c p) t -> p c t", p=128)
    for hh in range(8):
        ev = ld("gpsimd", xT3[:, :, hh * 256:(hh + 1) * 256],
                xTh3[:, :, hh * 256:(hh + 1) * 256], "g")
        if hh % 2 == 1:
            ev_xTs[hh // 2] = ev
        if hh == 3:
            ev_WT["k"] = ld("gpsimd", WTb[1].rearrange("p (c i) -> p c i", c=ND),
                            WT_h["k"].rearrange("(c p) i -> p c i", p=128), "g")
    ev_xT = ev_xTs[NTC - 1]
    ev_WT["qp"] = ld("gpsimd", WTq.rearrange("p (c i) -> p c i", c=ND),
                     WT_h["q"].rearrange("(c p) i -> p c i", p=128), "g")
    ev_consts_g = P.ev("g")

    bank_ev = [None, None, None, None]

    def proj_mm(wt_ap, ci, tt, bank, wait):
        ps = pspr[:, bank * 512:(bank + 1) * 512]

        def fn(e, wt=wt_ap, ci=ci, tt=tt, ps=ps):
            inst = None
            for cd in range(ND):
                inst = e.matmul(ps, wt[:, cd * D + ci * 128: cd * D + ci * 128 + 128],
                                xT[:, cd * T + tt * 512: cd * T + tt * 512 + 512],
                                start=(cd == 0), stop=(cd == ND - 1))
            return inst
        return P.emit("tensor", fn, inc="t", wait=wait), ps

    ev_scan_prev = None
    for ci in range(ND):
        for which in ("a", "k"):
            wt_ap = WTb[1] if which == "k" else WTb[0]
            for tt in range(NTC):
                bank = (ci * 8 + (0 if which == "k" else 4) + tt) % 4
                ev_mm, ps = proj_mm(wt_ap, ci, tt, bank,
                                    [ev_WT[which], ev_xTs[tt], ev_consts_q, bank_ev[bank]])
                if which == "k":
                    dst = wkT[:, ci * T + tt * 512: ci * T + tt * 512 + 512]
                    bank_ev[bank] = P.emit(
                        "vector",
                        lambda e, d=dst, p=ps, c=ci: e.tensor_scalar(
                            d, p, bk[:, c:c + 1], 0.0, op0=ALU.add, op1=ALU.max),
                        inc="v", wait=[ev_mm])
                else:
                    dst = Abuf[:, tt * 512:(tt + 1) * 512]
                    bank_ev[bank] = P.emit(
                        "scalar",
                        lambda e, d=dst, p=ps, c=ci: e.activation(
                            out=d, in_=p, func=AF.Sigmoid, bias=ba[:, c:c + 1]),
                        inc="s", wait=[ev_mm, ev_scan_prev if tt == 0 else None])
        ev_sig = P.ev("s")
        ev_eps_prev = P.emit("vector",
                             lambda e: e.tensor_scalar_add(Abuf[:], Abuf[:], 1e-6),
                             inc="v", wait=[ev_sig])
        ev_scan = P.emit("vector",
                         lambda e: e.tensor_tensor_scan(
                             Gbuf[:, ::-1], Abuf[:, ::-1], Abuf[:, ::-1], 1.0,
                             ALU.mult, ALU.bypass),
                         inc="v", wait=[ev_eps_prev])
        ev_scan_prev = ev_scan
        P.emit("vector",
               lambda e, c=ci: e.tensor_mul(wkT[:, c * T:(c + 1) * T],
                                            wkT[:, c * T:(c + 1) * T], Gbuf[:]),
               inc="v", wait=[ev_scan])
    ev_wk_done = P.ev("v")

    ev_ak_mm = P.ev("t")
    ev_WT["vp"] = ld("gpsimd", WTb[1].rearrange("p (c i) -> p c i", c=ND),
                     WT_h["v"].rearrange("(c p) i -> p c i", p=128), "g",
                     wait=[ev_ak_mm])

    for ci in range(ND):
        for tt in range(NTC):
            bank = (ci * 4 + tt) % 4
            ev_mm, ps = proj_mm(WTq, ci, tt, bank, [ev_WT["qp"], bank_ev[bank]])
            dst = qT[:, ci * T + tt * 512: ci * T + tt * 512 + 512]
            bank_ev[bank] = P.emit(
                "vector",
                lambda e, d=dst, p=ps, c=ci: e.tensor_scalar(
                    d, p, bq[:, c:c + 1], 0.0, op0=ALU.add, op1=ALU.max),
                inc="v", wait=[ev_mm])
    for ti in range(NT):
        for jc in range(2):
            bank = (ti * 2 + jc) % 4
            ps = pspr[:, bank * 512:(bank + 1) * 512]

            def fnv(e, ti=ti, jc=jc, ps=ps):
                for cd in range(ND):
                    e.matmul(ps, xT[:, cd * T + ti * 128: cd * T + ti * 128 + 128],
                             WTb[1][:, cd * D + jc * 512: cd * D + jc * 512 + 512],
                             start=(cd == 0), stop=False)
                return e.matmul(ps, onesrow[0:1, 0:128],
                                bvrow[0:1, jc * 512:(jc + 1) * 512],
                                start=False, stop=True)
            ev_mm = P.emit("tensor", fnv, inc="t",
                           wait=[ev_WT["vp"], bank_ev[bank]])
            dst = vS[:, ti * D + jc * 512: ti * D + jc * 512 + 512]
            bank_ev[bank] = P.emit(
                "vector",
                lambda e, d=dst, p=ps: e.tensor_copy(d, p),
                inc="v", wait=[ev_mm])
    ev_p1_s = P.ev("s")
    ev_p1_v = P.ev("v")
    DBG["qT"] = (qT, [128, ND * T], BF16, [P.ev("s"), P.ev("v")])
    DBG["wkT"] = (wkT, [128, ND * T], BF16, [ev_wk_done])
    DBG["v"] = (vS, [128, NT * D], BF16, [ev_p1_s, ev_p1_v])
    DBG["xTd"] = (xT, [128, ND * T], BF16, [ev_xT])
    if stop_after == "p1":
        return finish()

    for tc in range(NTC):
        for sc in range(4 * tc + 4):
            bank = slot(tc, sc) % 4
            ps = pspr[:, bank * 512:(bank + 1) * 512]

            def fns(e, tc=tc, sc=sc, ps=ps):
                inst = None
                for cd in range(ND):
                    inst = e.matmul(ps,
                                    wkT[:, cd * T + sc * 128: cd * T + sc * 128 + 128],
                                    qT[:, cd * T + tc * 512: cd * T + tc * 512 + 512],
                                    start=(cd == 0), stop=(cd == ND - 1))
                return inst
            ev_mm = P.emit("tensor", fns, inc="t",
                           wait=[ev_wk_done, ev_p1_v, bank_ev[bank]])
            dlt = sc - 4 * tc
            dst = scoresT[:, slot(tc, sc) * 512:(slot(tc, sc) + 1) * 512]
            if dlt >= 0:
                bank_ev[bank] = P.emit(
                    "vector",
                    lambda e, d=dst, p=ps, m=dlt: e.tensor_tensor(
                        d, p, masks[:, m * 512:(m + 1) * 512], ALU.mult),
                    inc="v", wait=[ev_mm])
            else:
                bank_ev[bank] = P.emit(
                    "scalar",
                    lambda e, d=dst, p=ps: e.activation(out=d, in_=p, func=AF.Copy),
                    inc="s", wait=[ev_mm])
    ev_sc_v = P.ev("v")
    ev_sc_s = P.ev("s")
    DBG["scoresT"] = (scoresT, [128, NSLOT * 512], BF16, [ev_sc_v, ev_sc_s])
    if stop_after == "p2a":
        return finish()

    ev_xres = [None] * (NT + 2)
    ev_y = [None] * NT
    ev_hbuf_used = None
    ev_numev = [[None, None], [None, None]]
    ev_den = [None, None]
    ev_sq = None
    ev_xres_used = [None, None]
    ev_b1m = P.emit("vector", lambda e: e.memset(b1b[:], 0.0), inc="v", wait=[])
    ev_b2m = P.emit("vector", lambda e: e.memset(b2b[:], 0.0), inc="v", wait=[])
    ev_b1 = ld("gpsimd", b1b[0:8, :], b1_h, "g", wait=[ev_b1m])
    ev_b2 = ld("gpsimd", b2b[0:8, :], b2_h, "g", wait=[ev_b2m])
    ev_2a_t = P.ev("t")
    for ti in range(2):
        ev_xres[ti] = ld("sync", xres[:, (ti % 2) * D:(ti % 2 + 1) * D],
                         x_h[ti * 128:(ti + 1) * 128, :], "q", wait=[ev_2a_t])
    ev_pool_mm = None
    pool_pending = []

    def emit_pool(ti):
        def fpool(e, ti=ti):
            e.matmul(pspr[0:1, 1024:1536], onescol[:, 0:1],
                     y_acc[:, ti * D: ti * D + 512],
                     start=(ti == 0), stop=(ti == NT - 1))
            return e.matmul(pspr[0:1, 1536:2048], onescol[:, 0:1],
                            y_acc[:, ti * D + 512: ti * D + 1024],
                            start=(ti == 0), stop=(ti == NT - 1))
        return P.emit("tensor", fpool, inc="t", wait=[ev_y[ti]])

    for ti in range(NT):
        tc = ti // 4
        par = ti % 2
        pnum = psnum if par == 0 else psx
        ls = lnsc if par == 0 else lnsc2
        pden = pspr[:, 0:1] if par == 0 else pspr[:, 512:513]

        def fnum(e, ti=ti, tc=tc, jc=0, pnum=pnum):
            inst = None
            for sc in range(ti + 1):
                inst = e.matmul(pnum[:, jc * 512:(jc + 1) * 512],
                                scoresT[:, slot(tc, sc) * 512 + (ti % 4) * 128:
                                        slot(tc, sc) * 512 + (ti % 4) * 128 + 128],
                                vS[:, sc * D + jc * 512: sc * D + jc * 512 + 512],
                                start=(sc == 0), stop=(sc == ti))
            return inst
        w0 = [ev_sc_v, ev_sc_s] if ti == 0 else []
        ev_n0 = P.emit("tensor", lambda e, f=fnum, ls=ls: f(e, jc=0),
                       inc="t", wait=w0 + [ev_numev[par][0]])
        ev_n1 = P.emit("tensor", lambda e, f=fnum, ls=ls: f(e, jc=1),
                       inc="t", wait=[ev_numev[par][1]])

        def fden(e, ti=ti, tc=tc, pden=pden):
            inst = None
            for sc in range(ti + 1):
                inst = e.matmul(pden,
                                scoresT[:, slot(tc, sc) * 512 + (ti % 4) * 128:
                                        slot(tc, sc) * 512 + (ti % 4) * 128 + 128],
                                onescol_bf[:, 0:1],
                                start=(sc == 0), stop=(sc == ti))
            return inst
        ev_dn = P.emit("tensor", fden, inc="t", wait=[ev_den[par], ev_consts_g])
        if len(pool_pending) > 2:
            ev_pool_mm = emit_pool(pool_pending.pop(0))
        ev_rcp_i = P.emit("vector",
                          lambda e, pden=pden, ls=ls: e.tensor_scalar_add(
                              ls[:, 0:1], pden, cst[:, 0:1]),
                          inc="v", wait=[ev_dn])
        ev_den[par] = ev_rcp_i
        ev_rcp = P.emit("vector",
                        lambda e, ls=ls: e.reciprocal(ls[:, 1:2], ls[:, 0:1]),
                        inc="v", wait=[ev_rcp_i])
        xs = xres[:, (ti % 2) * D:(ti % 2 + 1) * D]
        ev_h0 = P.emit("vector",
                       lambda e, xs=xs, pnum=pnum, ls=ls: e.scalar_tensor_tensor(
                           out=hbuf[:, 0:512], in0=pnum[:, 0:512], scalar=ls[:, 1:2],
                           in1=xs[:, 0:512], op0=ALU.mult, op1=ALU.add),
                       inc="v", wait=[ev_n0, ev_rcp, ev_xres[ti], ev_hbuf_used])
        ev_numev[par][0] = ev_h0
        ev_h1 = P.emit("vector",
                       lambda e, xs=xs, pnum=pnum, ls=ls: e.scalar_tensor_tensor(
                           out=hbuf[:, 512:1024], in0=pnum[:, 512:1024],
                           scalar=ls[:, 1:2],
                           in1=xs[:, 512:1024], op0=ALU.mult, op1=ALU.add),
                       inc="v", wait=[ev_n1])
        ev_numev[par][1] = ev_h1
        ev_xres_used[ti % 2] = ev_h1
        if ti + 2 < NT:
            ev_xres[ti + 2] = ld("sync", xres[:, ((ti + 2) % 2) * D:((ti + 2) % 2 + 1) * D],
                                 x_h[(ti + 2) * 128:(ti + 3) * 128, :], "q",
                                 wait=[ev_xres_used[ti % 2]])
        ev_sum = P.emit("vector",
                        lambda e, ls=ls: e.reduce_sum(ls[:, 2:3], hbuf[:], axis=AX.X),
                        inc="v", wait=[ev_h1])
        ev_sq = P.emit("scalar",
                       lambda e, ls=ls: e.activation(out=sqd[:], in_=hbuf[:], func=AF.Square,
                                              accum_out=ls[:, 3:4]),
                       inc="s", wait=[ev_h1, ev_sq])
        ev_m = P.emit("vector",
                      lambda e, ls=ls: e.tensor_scalar_mul(ls[:, 4:5], ls[:, 2:3], 1.0 / D),
                      inc="v", wait=[ev_sum])
        ev_v1 = P.emit("vector",
                       lambda e, ls=ls: e.tensor_scalar_mul(ls[:, 5:6], ls[:, 3:4], 1.0 / D),
                       inc="v", wait=[ev_sq])
        ev_v2 = P.emit("vector",
                       lambda e, ls=ls: e.tensor_mul(ls[:, 6:7], ls[:, 4:5], ls[:, 4:5]),
                       inc="v", wait=[ev_m])
        ev_v3 = P.emit("vector",
                       lambda e, ls=ls: e.tensor_sub(ls[:, 7:8], ls[:, 5:6], ls[:, 6:7]),
                       inc="v", wait=[ev_v1, ev_v2])
        ev_l = P.emit("scalar",
                      lambda e, ls=ls: e.activation(out=ls[:, 5:6], in_=ls[:, 7:8],
                                             func=AF.Ln, bias=cst[:, 1:2]),
                      inc="s", wait=[ev_v3])
        ev_r = P.emit("scalar",
                      lambda e, ls=ls: e.activation(out=ls[:, 6:7], in_=ls[:, 5:6],
                                             func=AF.Exp, scale=cst[:, 5:6]),
                      inc="s", wait=[ev_l])
        ev_y[ti] = P.emit("vector",
                          lambda e, ti=ti, ls=ls: e.tensor_scalar(
                              y_acc[:, ti * D:(ti + 1) * D], hbuf[:],
                              ls[:, 4:5], ls[:, 6:7],
                              op0=ALU.subtract, op1=ALU.mult),
                          inc="v", wait=[ev_r])
        ev_hbuf_used = ev_y[ti]
        pool_pending.append(ti)
        if stop_after == "t0" and ti == 0:
            DBG["hbuf"] = (hbuf, [128, D], F32, [ev_y[0]])
            DBG["lnsc"] = (lnsc, [128, 8], F32, [ev_y[0]])
            DBG["xres0"] = (xres, [128, 2 * D], F32, [ev_y[0]])
            DBG["y1t0"] = (y_acc, [128, NT * D], F32, [ev_y[0]])
            return finish()

    while pool_pending:
        ev_pool_mm = emit_pool(pool_pending.pop(0))

    DBG["y1"] = (y_acc, [128, NT * D], F32, [ev_y[NT - 1]])
    if stop_after == "p2b":
        return finish()

    ev_pooled = P.emit("vector",
                       lambda e: e.tensor_scalar_mul(outst[0:1, 0:1024],
                                                     pspr[0:1, 1024:2048], 1.0 / T),
                       inc="v", wait=[ev_pool_mm])

    def ftrp(e):
        inst = None
        for c in range(ND):
            inst = e.matmul(pspr[:, 1024 + c: 1025 + c],
                            outst[0:1, c * 128:(c + 1) * 128],
                            onesrow[0:1, 0:1],
                            start=(c == 0), stop=(c == ND - 1))
        return inst
    ev_poolT = P.emit("tensor", ftrp, inc="t", wait=[ev_pooled])
    ev_poolT_sb = P.emit("vector",
                         lambda e: e.tensor_copy(rcol[:, 8:16], pspr[:, 1024:1032]),
                         inc="v", wait=[ev_poolT])

    def flog(e):
        inst = None
        for c in range(ND):
            inst = e.matmul(pspr[0:1, 1536:1544], rcol[:, 8 + c: 9 + c],
                            Wr[:, c * 8:(c + 1) * 8],
                            start=(c == 0), stop=(c == ND - 1))
        return inst
    ev_logmm = P.emit("tensor", flog, inc="t", wait=[ev_poolT_sb])
    ev_log = P.emit("vector", lambda e: e.tensor_copy(rsc[0:1, 0:8], pspr[0:1, 1536:1544]),
                    inc="v", wait=[ev_logmm])
    e1 = P.emit("vector", lambda e: e.reduce_max(rsc[0:1, 8:9], rsc[0:1, 0:8], axis=AX.X),
                inc="v", wait=[ev_log])
    e2_ = P.emit("vector", lambda e: e.tensor_scalar(rmsk[0:1, :], rsc[0:1, 0:8],
                                                     rsc[0:1, 8:9], None, op0=ALU.is_equal),
                 inc="v", wait=[e1])
    e3 = P.emit("vector", lambda e: e.select(rsc[0:1, 24:32], rmsk[0:1, :],
                                             i8r[0:1, :], bigr[0:1, :]),
                inc="v", wait=[e2_])
    e4 = P.emit("vector", lambda e: e.tensor_reduce(rsc[0:1, 9:10], rsc[0:1, 24:32],
                                                    axis=AX.X, op=ALU.min),
                inc="v", wait=[e3])
    e5 = P.emit("vector", lambda e: e.tensor_scalar(rsc[0:1, 16:24], i8r[0:1, :],
                                                    rsc[0:1, 9:10], None, op0=ALU.is_equal),
                inc="v", wait=[e4])
    e6 = P.emit("vector", lambda e: e.tensor_scalar(rsc[0:1, 32:40], rsc[0:1, 16:24],
                                                    -1e30, None, op0=ALU.mult),
                inc="v", wait=[e5])
    e7 = P.emit("vector", lambda e: e.tensor_add(rsc[0:1, 32:40], rsc[0:1, 32:40],
                                                 rsc[0:1, 0:8]),
                inc="v", wait=[e6])
    e8 = P.emit("vector", lambda e: e.reduce_max(rsc[0:1, 10:11], rsc[0:1, 32:40],
                                                 axis=AX.X),
                inc="v", wait=[e7])
    e9 = P.emit("vector", lambda e: e.tensor_scalar(rmsk[0:1, :], rsc[0:1, 32:40],
                                                    rsc[0:1, 10:11], None,
                                                    op0=ALU.is_equal),
                inc="v", wait=[e8])
    e10 = P.emit("vector", lambda e: e.select(rsc[0:1, 24:32], rmsk[0:1, :],
                                              i8r[0:1, :], bigr[0:1, :]),
                 inc="v", wait=[e9])
    e11 = P.emit("vector", lambda e: e.tensor_reduce(rsc[0:1, 11:12], rsc[0:1, 24:32],
                                                     axis=AX.X, op=ALU.min),
                 inc="v", wait=[e10])
    e12 = P.emit("vector", lambda e: e.tensor_sub(rsc[0:1, 40:41], rsc[0:1, 10:11],
                                                  rsc[0:1, 8:9]),
                 inc="v", wait=[e11])
    e13 = P.emit("scalar", lambda e: e.activation(out=rsc[0:1, 12:13],
                                                  in_=rsc[0:1, 40:41], func=AF.Exp),
                 inc="s", wait=[e12])
    e14 = P.emit("vector", lambda e: e.tensor_scalar_add(rsc[0:1, 13:14],
                                                         rsc[0:1, 12:13], 1.0),
                 inc="v", wait=[e13])
    e15 = P.emit("vector", lambda e: e.reciprocal(rsc[0:1, 14:15], rsc[0:1, 13:14]),
                 inc="v", wait=[e14])
    e16 = P.emit("vector", lambda e: e.tensor_mul(rsc[0:1, 15:16], rsc[0:1, 12:13],
                                                  rsc[0:1, 14:15]),
                 inc="v", wait=[e15])

    ev_trb = [ev_numev[0][1], ev_numev[1][1]]
    ev_x1b_used = None
    for ti in range(8):
        ev_cp = P.emit("scalar",
                       lambda e, ti=ti: e.activation(
                           out=x1btmp[:], in_=y_acc[:, ti * D:(ti + 1) * D],
                           func=AF.Copy),
                       inc="s", wait=[ev_y[ti], ev_x1b_used])
        pb = ti % 2
        pst = psnum if pb == 0 else psx

        def ftr(e, pst=pst):
            inst = None
            for c in range(ND):
                inst = e.matmul(pst[:, c * 128:(c + 1) * 128],
                                x1btmp[:, c * 128:(c + 1) * 128], ident_sb[:],
                                start=(c in (0, 4)), stop=(c in (3, 7)))
            return inst
        ev_tr = P.emit("tensor", ftr, inc="t", wait=[ev_cp, ev_ident, ev_trb[pb]])
        dst = x1T.rearrange("p (c t) -> p c t", c=ND)[:, :, ti * 128:(ti + 1) * 128]
        src_ = pst.rearrange("p (c t) -> p c t", c=ND)
        ev_x1b_used = ev_tr
        ev_trb[pb] = P.emit("vector",
                            lambda e, d=dst, s=src_: e.tensor_copy(d, s),
                            inc="v", wait=[ev_tr])
    ev_x1T_all_v = P.ev("v")

    def fbc(e):
        srcs = [rsc[0:1, 9:10], rsc[0:1, 11:12], rsc[0:1, 14:15], rsc[0:1, 15:16]]
        inst = None
        for j, s in enumerate(srcs):
            inst = e.matmul(pspr[:, j:j + 1], onesrow[0:1, 0:128], s,
                            start=(j == 0), stop=(j == 3))
        return inst
    ev_bc = P.emit("tensor", fbc, inc="t", wait=[e16, ev_den[0], ev_den[1]])
    ev_bc_sb = P.emit("vector", lambda e: e.tensor_copy(rcol[:, 0:4], pspr[:, 0:4]),
                      inc="v", wait=[ev_bc])

    def fidx(e):
        e.tensor_scalar(rcol[:, 4:5], rcol[:, 0:1], 1024.0, None, op0=ALU.mult)
        e.tensor_scalar(rcol[:, 5:6], rcol[:, 1:2], 1024.0, None, op0=ALU.mult)
        e.tensor_scalar(rcol[:, 6:7], rcol[:, 0:1], 4096.0, None, op0=ALU.mult)
        return e.tensor_scalar(rcol[:, 7:8], rcol[:, 1:2], 4096.0, None, op0=ALU.mult)
    ev_idx0 = P.emit("vector", fidx, inc="v", wait=[ev_bc_sb])

    def fidx2(e):
        inst = None
        for k in range(2):
            e.tensor_tensor(ridx1[:, k * 8:(k + 1) * 8],
                            rcol[:, 4 + k:5 + k].to_broadcast([128, 8]), iw1[:], ALU.add)
        for k in range(2):
            inst = e.tensor_tensor(ridx2[:, k * 32:(k + 1) * 32],
                                   rcol[:, 6 + k:7 + k].to_broadcast([128, 32]), iw2[:],
                                   ALU.add)
        return inst
    ev_idx = P.emit("vector", fidx2, inc="v", wait=[ev_idx0])

    ev_gath_w1 = [None] * 8

    def gather_block(blk, waits):
        k, qq = blk // 4, blk % 4
        bp = blk % 2
        evs = []
        for cd in range(ND):
            def fg(e, k=k, qq=qq, bp=bp, cd=cd):
                return e.indirect_dma_start(
                    out=W1q[bp][:, cd * 1024:(cd + 1) * 1024],
                    out_offset=None,
                    in_=W1flat,
                    in_offset=IndirectOffsetOnAxis(ridx1[:, k * 8 + cd: k * 8 + cd + 1], 0),
                    element_offset=qq * 1024)
            evs.append(P.emit("gpsimd", fg, inc="g", dma=True, wait=waits))
            waits = []
        ev_gath_w1[blk] = evs[-1]
        for fc in range(8):
            def fg2(e, k=k, qq=qq, bp=bp, fc=fc):
                return e.indirect_dma_start(
                    out=W2q[bp][:, fc * 1024:(fc + 1) * 1024],
                    out_offset=None,
                    in_=W2flat,
                    in_offset=IndirectOffsetOnAxis(
                        ridx2[:, k * 32 + qq * 8 + fc: k * 32 + qq * 8 + fc + 1], 0),
                    element_offset=0)
            evs.append(P.emit("gpsimd", fg2, inc="g", dma=True, wait=[]))
        return evs[-1]

    ev_gath = [None] * 8
    ev_blk_t = [None] * 8
    ev_gath[0] = gather_block(0, [ev_idx])
    ev_gath[1] = gather_block(1, [])

    for ti in range(8, NT):
        ev_cp = P.emit("scalar",
                       lambda e, ti=ti: e.activation(
                           out=x1btmp[:], in_=y_acc[:, ti * D:(ti + 1) * D],
                           func=AF.Copy),
                       inc="s", wait=[ev_y[ti], ev_x1b_used])
        pb = ti % 2
        pst = psnum if pb == 0 else psx

        def ftr(e, pst=pst):
            inst = None
            for c in range(ND):
                inst = e.matmul(pst[:, c * 128:(c + 1) * 128],
                                x1btmp[:, c * 128:(c + 1) * 128], ident_sb[:],
                                start=(c in (0, 4)), stop=(c in (3, 7)))
            return inst
        ev_tr = P.emit("tensor", ftr, inc="t", wait=[ev_cp, ev_ident, ev_trb[pb]])
        dst = x1T.rearrange("p (c t) -> p c t", c=ND)[:, :, ti * 128:(ti + 1) * 128]
        src_ = pst.rearrange("p (c t) -> p c t", c=ND)
        ev_x1b_used = ev_tr
        ev_trb[pb] = P.emit("vector",
                            lambda e, d=dst, s=src_: e.tensor_copy(d, s),
                            inc="v", wait=[ev_tr])
    ev_x1T_all_v = P.ev("v")

    ev_ohm = P.emit("vector", lambda e: e.memset(oh8[:], 0.0), inc="v", wait=[])
    ev_oh0 = P.emit("vector",
                    lambda e: e.tensor_scalar(oh8[0:8, 0:1], icol[0:8, :],
                                              rcol[0:8, 0:1], None, op0=ALU.is_equal),
                    inc="v", wait=[ev_bc_sb, ev_ohm])
    ev_oh1 = P.emit("vector",
                    lambda e: e.tensor_scalar(oh8[0:8, 1:2], icol[0:8, :],
                                              rcol[0:8, 1:2], None, op0=ALU.is_equal),
                    inc="v", wait=[ev_bc_sb, ev_ohm])
    ev_g8a = P.emit("vector",
                    lambda e: e.tensor_scalar(oh8[0:8, 2:3], oh8[0:8, 0:1],
                                              rcol[0:8, 2:3], None, op0=ALU.mult),
                    inc="v", wait=[ev_oh0])
    ev_g8b = P.emit("vector",
                    lambda e: e.scalar_tensor_tensor(
                        out=oh8[0:8, 3:4], in0=oh8[0:8, 1:2], scalar=rcol[0:8, 3:4],
                        in1=oh8[0:8, 2:3], op0=ALU.mult, op1=ALU.add),
                    inc="v", wait=[ev_oh1, ev_g8a])
    def fb2(e):
        e.matmul(psnum[0:1, 0:512], oh8[:, 3:4], b2b[:, 0:512], start=True, stop=True)
        return e.matmul(psnum[0:1, 512:1024], oh8[:, 3:4], b2b[:, 512:1024],
                        start=True, stop=True)
    ev_b2mm = P.emit("tensor", fb2, inc="t", wait=[ev_g8b, ev_b2, ev_trb[0]])
    ev_b2row = P.emit("vector",
                      lambda e: e.tensor_copy(b2row[0:1, :], psnum[0:1, 0:1024]),
                      inc="v", wait=[ev_b2mm])

    def fb1(e):
        inst = None
        for k in range(2):
            for fc in range(32):
                inst = e.matmul(psx[:, k * 32 + fc: k * 32 + fc + 1],
                                b1b[:, fc * 128:(fc + 1) * 128], oh8[:, k:k + 1],
                                start=(k == 0 and fc == 0), stop=(k == 1 and fc == 31))
        return inst
    ev_b1mm = P.emit("tensor", fb1, inc="t", wait=[ev_oh1, ev_b1, ev_trb[1]])
    ev_b1sel = P.emit("vector",
                      lambda e: e.tensor_copy(b1selT[:], psx[:, 0:64]),
                      inc="v", wait=[ev_b1mm])

    DBG["router"] = (rsc, [1, 64], F32, [e16, e4, e11])
    DBG["rcol"] = (rcol, [128, 16], F32, [ev_idx])
    DBG["x1T"] = (x1T, [128, ND * T], BF16, [ev_x1T_all_v])
    DBG["W1qA"] = (W1q[0], [128, ND * 1024], BF16, [ev_gath[0]])
    DBG["W2qA"] = (W2q[0], [128, 8 * D], BF16, [ev_gath[0]])
    DBG["b1selT"] = (b1selT, [128, 64], F32, [ev_b1sel])
    DBG["b2row"] = (b2row, [1, D], BF16, [ev_b2row])
    if stop_after == "pr":
        return finish()

    ev_out_sq = None
    ev_sq_prev = None

    def emit_tl(ti):
        tb = ti // 2
        P.emit("vector",
               lambda e, ti=ti: e.reduce_sum(
                   lnS[:, ti:ti + 1], y_acc[:, ti * D:(ti + 1) * D], axis=AX.X),
               inc="v", wait=[ev_yv[tb]])
        return P.emit("scalar",
                      lambda e, ti=ti: e.activation(
                          out=sqd2[:], in_=y_acc[:, ti * D:(ti + 1) * D],
                          func=AF.Square, accum_out=lnQ[:, ti:ti + 1]),
                      inc="s", wait=[ev_yv[tb], ev_sq_prev])

    ev_st_used = [None, None]

    def emit_ln2_group(lo, hi, wait_ev):
        ev_m16 = P.emit("vector",
                        lambda e: e.tensor_scalar_mul(lnM[:, lo:hi], lnS[:, lo:hi], 1.0 / D),
                        inc="v", wait=[wait_ev])
        ev_q16 = P.emit("vector",
                        lambda e: e.tensor_scalar_mul(lnV[:, lo:hi], lnQ[:, lo:hi], 1.0 / D),
                        inc="v", wait=[wait_ev])
        ev_mm16 = P.emit("vector",
                         lambda e: e.tensor_mul(lnS[:, lo:hi], lnM[:, lo:hi], lnM[:, lo:hi]),
                         inc="v", wait=[ev_m16])
        ev_v16 = P.emit("vector",
                        lambda e: e.tensor_sub(lnV[:, lo:hi], lnV[:, lo:hi], lnS[:, lo:hi]),
                        inc="v", wait=[ev_q16, ev_mm16])
        ev_l16 = P.emit("scalar",
                        lambda e: e.activation(out=lnS[:, lo:hi], in_=lnV[:, lo:hi],
                                               func=AF.Ln, bias=cst[:, 1:2]),
                        inc="s", wait=[ev_v16])
        ev_r16 = P.emit("scalar",
                        lambda e: e.activation(out=lnQ[:, lo:hi], in_=lnS[:, lo:hi],
                                               func=AF.Exp, scale=cst[:, 5:6]),
                        inc="s", wait=[ev_l16])
        for ti in range(lo, hi):
            par = ti % 2
            ev_o = P.emit("vector",
                          lambda e, ti=ti, par=par: e.tensor_scalar(
                              outst[:, par * D:(par + 1) * D],
                              y_acc[:, ti * D:(ti + 1) * D],
                              lnM[:, ti:ti + 1], lnQ[:, ti:ti + 1],
                              op0=ALU.subtract, op1=ALU.mult),
                          inc="v", wait=[ev_r16, ev_st_used[par]])
            ev_st_used[par] = ld("sync", out_h[ti * 128:(ti + 1) * 128, :],
                                 outst[:, par * D:(par + 1) * D], "q", wait=[ev_o])

    ev_hsb_used = [None, None]
    ev_hps_free = [ev_b2row, ev_b1sel]
    ev_ylast = ev_bc_sb
    ev_yv = [None] * 8
    for blk in range(8):
        k, qq = blk // 4, blk % 4
        bp = blk % 2
        for tb in range(8):
            ev_gelu_g = [None] * 4

            def emit_A(g):
                par = g % 2
                pst = psnum if par == 0 else psx
                hs = hsb[par]

                def fA(e, k=k, qq=qq, bp=bp, tb=tb, g=g, pst=pst):
                    inst = None
                    for fcl in range(2):
                        fc = g * 2 + fcl
                        od = pst[:, fcl * 512: fcl * 512 + 256]
                        for cd in range(ND):
                            inst = e.matmul(
                                od,
                                W1q[bp][:, cd * 1024 + fc * 128: cd * 1024 + fc * 128 + 128],
                                x1T[:, cd * T + tb * 256: cd * T + tb * 256 + 256],
                                start=(cd == 0), stop=(cd == ND - 1))
                    return inst
                wA = [ev_gath_w1[blk] if (g == 0 and tb == 0) else None,
                      ev_hps_free[par], ev_b1sel]
                if blk == 0 and tb == 0 and g == 0:
                    wA.append(ev_x1T_all_v)
                ev_A = P.emit("tensor", fA, inc="t", wait=wA)
                bcol0 = k * 32 + qq * 8 + g * 2
                P.emit("scalar",
                       lambda e, hs=hs, pst=pst, bc=bcol0: e.activation(
                           out=hs[:, 0:256], in_=pst[:, 0:256], func=AF.Gelu,
                           bias=b1selT[:, bc:bc + 1]),
                       inc="s", wait=[ev_A, ev_hsb_used[par]])
                ev_gelu_g[g] = P.emit(
                    "scalar",
                    lambda e, hs=hs, pst=pst, bc=bcol0: e.activation(
                        out=hs[:, 256:512], in_=pst[:, 512:768], func=AF.Gelu,
                        bias=b1selT[:, bc + 1:bc + 2]),
                    inc="s", wait=[])
                ev_hps_free[par] = ev_gelu_g[g]

            def emit_B(g):
                nonlocal ev_ylast
                par = g % 2
                hs = hsb[par]

                def fB(e, k=k, qq=qq, bp=bp, tb=tb, g=g, blk=blk, hs=hs):
                    inst = None
                    for fcl in range(2):
                        fc = g * 2 + fcl
                        first = (g == 0 and fcl == 0)
                        last = (g == 3 and fcl == 1)
                        for tl in range(2):
                            for jc in range(2):
                                od = pspr[:, (tl * 2 + jc) * 512:(tl * 2 + jc) * 512 + 512]
                                if blk == 0 and first:
                                    e.matmul(od, onesrow_bf[0:1, 0:128],
                                             b2row[0:1, jc * 512:(jc + 1) * 512],
                                             start=True, stop=False)
                                inst = e.matmul(
                                    od,
                                    hs[:, fcl * 256 + tl * 128: fcl * 256 + tl * 128 + 128],
                                    W2q[bp][:, fc * 1024 + jc * 512: fc * 1024 + jc * 512 + 512],
                                    start=(first and blk != 0), stop=last)
                    return inst
                ev_B = P.emit("tensor", fB, inc="t",
                              wait=[ev_gelu_g[g], ev_ylast if g == 0 else None,
                                    ev_gath[blk] if (g == 0 and tb == 0) else None,
                                    ev_b2row])
                ev_hsb_used[par] = ev_B
                return ev_B

            emit_A(0)
            emit_A(1)
            emit_B(0)
            emit_A(2)
            emit_B(1)
            emit_A(3)
            emit_B(2)
            ev_B = emit_B(3)
            wk_col = rcol[:, 2 + k:3 + k]

            def fy(e, tb=tb, wk_col=wk_col):
                inst = None
                for tl in range(2):
                    for jc in range(2):
                        od = pspr[:, (tl * 2 + jc) * 512:(tl * 2 + jc) * 512 + 512]
                        dst = y_acc[:, (tb * 2 + tl) * D + jc * 512:
                                    (tb * 2 + tl) * D + jc * 512 + 512]
                        inst = e.scalar_tensor_tensor(out=dst, in0=od, scalar=wk_col,
                                                      in1=dst, op0=ALU.mult, op1=ALU.add)
                return inst
            ev_yv[tb] = P.emit("vector", fy, inc="v", wait=[ev_B])
            ev_ylast = ev_yv[tb]
            if blk == 7:
                emit_tl(2 * tb)
                ev_out_sq = emit_tl(2 * tb + 1)
                ev_sq_prev = ev_out_sq
        ev_blk_t[blk] = P.ev("t")
        if blk + 2 < 8:
            ev_gath[blk + 2] = gather_block(blk + 2, [ev_blk_t[blk]])

    DBG["y2"] = (y_acc, [128, NT * D], F32, [ev_yv[7], P.ev("v"), P.ev("t")])
    if stop_after == "p3":
        return finish()

    if stop_after != "raw":
        emit_ln2_group(0, 12, ev_out_sq)
        emit_ln2_group(12, 16, ev_out_sq)
    else:
        ev_st_used = [None, None]
        for ti in range(NT):
            ev_st_used[0] = ld("sync", out_h[ti * 128:(ti + 1) * 128, :],
                               y_acc[:, ti * D:(ti + 1) * D], "q",
                               wait=[ev_out_sq, ev_st_used[0]])
    DBG["lnS"] = (lnS, [128, 16], F32, [P.ev("v"), P.ev("s")])
    DBG["lnQ"] = (lnQ, [128, 16], F32, [P.ev("v"), P.ev("s")])
    DBG["lnM"] = (lnM, [128, 16], F32, [P.ev("v"), P.ev("s")])
    DBG["lnV"] = (lnV, [128, 16], F32, [P.ev("v"), P.ev("s")])

    return finish()


def make_consts():
    cst = np.zeros((128, 8), np.float32)
    cst[:, 0] = 1e-6
    cst[:, 1] = 1e-5
    cst[:, 2] = 1.0 / D
    cst[:, 3] = 1.0 / T
    cst[:, 4] = -1.0
    cst[:, 5] = -0.5
    cst[:, 6] = 1.0
    p = np.arange(128, dtype=np.uint32)
    iw1 = (p[:, None] + 128 * np.arange(8, dtype=np.uint32)[None, :]).astype(np.uint32)
    iw2 = np.zeros((128, 32), np.uint32)
    for qq in range(4):
        for fc in range(8):
            iw2[:, qq * 8 + fc] = qq * 1024 + fc * 128 + p
    masks = np.zeros((128, 4, 512), ml_dtypes.bfloat16)
    f = np.arange(512)
    for dlt in range(4):
        masks[:, dlt, :] = (f[None, :] - np.arange(128)[:, None] >= 128 * dlt)
    return {
        "ident": np.eye(128, dtype=ml_dtypes.bfloat16),
        "masks": masks.reshape(128, 2048),
        "cst": cst,
        "iotaW1": iw1,
        "iotaW2": iw2,
        "iota8row": np.arange(8, dtype=np.float32).reshape(1, 8),
        "iotacol": np.arange(128, dtype=np.float32).reshape(128, 1),
        "bigrow": np.full((1, 8), 1e9, np.float32),
        "onespack": np.ones((128, 4), np.float32),
        "onesrow": np.ones((1, 512), np.float32),
    }


def make_in_maps(inputs):
    consts = make_consts()
    xs = np.asarray(inputs["L0_attn_x_seq"])
    shared = dict(consts)
    for p, key in (("q", "L0_attn_W_q"), ("k", "L0_attn_W_k"),
                   ("v", "L0_attn_W_v"), ("a", "L0_attn_W_a")):
        shared[f"W{p}T"] = np.ascontiguousarray(np.asarray(inputs[key]).T)
    for p, key in (("q", "L0_attn_b_q"), ("k", "L0_attn_b_k"), ("a", "L0_attn_b_a")):
        shared[f"b{p}128"] = np.ascontiguousarray(
            np.asarray(inputs[key]).reshape(8, 128).T)
    shared["bvrow"] = np.asarray(inputs["L0_attn_b_v"]).reshape(1, D)
    shared["Wr128"] = np.ascontiguousarray(
        np.asarray(inputs["L0_ffn_Wr"]).reshape(8, 128, 8).transpose(1, 0, 2).reshape(128, 64))
    shared["b1"] = np.asarray(inputs["L0_ffn_b1"])
    shared["b2"] = np.asarray(inputs["L0_ffn_b2"])
    shared["W1"] = np.asarray(inputs["L0_ffn_W1"])
    shared["W2"] = np.asarray(inputs["L0_ffn_W2"])
    in_maps = []
    for b in range(8):
        m = dict(shared)
        m["x"] = np.ascontiguousarray(xs[b])
        m["xT"] = np.ascontiguousarray(xs[b].T)
        in_maps.append(m)
    return in_maps


def kernel(**inputs):
    nc = build_nc()
    in_maps = make_in_maps(inputs)
    res = run_bass_kernel_spmd(nc, in_maps, core_ids=list(range(8)))
    out = np.stack([np.asarray(r["out"]) for r in res.results])
    return out.astype(np.float32)
